# version 08
# baseline (speedup 1.0000x reference)
import contextlib
import math
import numpy as np
import ml_dtypes
import concourse.bass as bass
import concourse.mybir as mybir
from concourse.bass_utils import run_bass_kernel_spmd

F32 = mybir.dt.float32
BF16 = mybir.dt.bfloat16
I8 = mybir.dt.int8
AF = mybir.ActivationFunctionType
ALU = mybir.AluOpType
AX = mybir.AxisListType

B, S, DM = 2, 4096, 4096
H, HKV, HD = 32, 8, 128
FF = 11008
NCORE = 8
TPC = 1024
NJ = 8
NST = 2
STW = 512
DC = DM // 128
IC = FF // 128
GQ = H // HKV
LN_EPS = 1e-5
RMS_EPS = 1e-6
MASKVAL = -1e33

_AS = 2.0 ** (-(2.0 ** -(math.log2(H) - 3)))
SLOPES = [_AS * _AS ** i for i in range(H)]

ENGINES = ("sync", "gpsimd", "tensor", "vector", "scalar")
ATT_DEBUG = False
ALL_PHASES = ("ffn1", "qkv", "gather", "attn", "oproj", "ffn2")


class Prog:
    def __init__(self, ctx):
        self.ctx = ctx
        self.ops = {e: [] for e in ENGINES}
        self.trace = {e: [] for e in ENGINES}

    def wait(self, eng, sem, val):
        if val is None or val <= 0:
            return
        h = self.ctx.sem[sem]
        self.trace[eng].append(("w", sem, val))
        self.ops[eng].append(lambda e, h=h, v=val: e.wait_ge(h, v))

    def emit(self):
        self.ctx.traces.append(self.trace)
        with self.ctx.nc.Block() as b:
            for eng in ENGINES:
                lst = self.ops[eng]
                if not lst:
                    continue

                def body(e, lst=lst):
                    for f in lst:
                        f(e)
                getattr(b, eng)(body)


class Ring:

    def __init__(self, ctx, stack, name, nslots):
        self.ctx = ctx
        self.name = name
        self.slots = []
        for i in range(nslots):
            h = stack.enter_context(ctx.nc.semaphore(f"{name}{i}"))
            self.slots.append([h, 0])
        self.i = 0
        ctx.rings.append(self)

    def issue(self, p, eng, make, by=16, advance=True):
        sl = self.i % len(self.slots)
        if advance:
            self.i += 1
        self.slots[sl][1] += by
        h, c = self.slots[sl][0], self.slots[sl][1]
        p.trace[eng].append(("i", f"{self.name}{sl}", c))
        p.ops[eng].append(lambda e, mk=make, h=h, by=by: mk(e).then_inc(h, by))
        return (sl, c)

    def wait(self, p, eng, token):
        sl, c = token
        h = self.slots[sl][0]
        p.trace[eng].append(("w", f"{self.name}{sl}", c))
        p.ops[eng].append(lambda e, h=h, v=c: e.wait_ge(h, v))

    def waiter(self, token):
        return lambda p, eng, r=self, t=token: r.wait(p, eng, t)


def sem_waiter(sem, val):
    return lambda p, eng, sem=sem, val=val: p.wait(eng, sem, val)


class Ctx:
    def __init__(self, nc):
        self.nc = nc
        self.sem = {}
        self.cnt = {}
        self.t = {}
        self.dram = {}
        self.traces = []
        self.rings = []
        self.r = {}

    def check_deadlock(self):
        streams = {e: [] for e in ENGINES}
        for tr in self.traces:
            for e in ENGINES:
                streams[e].extend(tr[e])
        posn = {e: 0 for e in ENGINES}
        cnt = {}
        progress = True
        while progress:
            progress = False
            for e in ENGINES:
                while posn[e] < len(streams[e]):
                    kind, sem, val = streams[e][posn[e]]
                    if kind == "i":
                        cnt[sem] = max(cnt.get(sem, 0), val)
                        posn[e] += 1
                        progress = True
                    else:
                        if cnt.get(sem, 0) >= val:
                            posn[e] += 1
                            progress = True
                        else:
                            break
        stuck = {e: posn[e] for e in ENGINES if posn[e] < len(streams[e])}
        if stuck:
            detail = {e: streams[e][posn[e]] for e in stuck}
            raise RuntimeError(f"DEADLOCK: engines stuck at waits {detail}; "
                               f"counters {cnt}")
        return True

    def val(self, sem):
        return self.cnt[sem]


def op(p, ctx, eng, make, inc=None, by=1):
    if inc is None:
        p.ops[eng].append(lambda e, mk=make: mk(e))
        return None
    ctx.cnt[inc] += by
    h = ctx.sem[inc]
    p.trace[eng].append(("i", inc, ctx.cnt[inc]))
    p.ops[eng].append(lambda e, mk=make, h=h, by=by: mk(e).then_inc(h, by))
    return ctx.cnt[inc]


def barrier(p, ctx, engines=ENGINES):
    for eng in engines:
        for s in ("ld", "wq", "st", "pe", "dv", "ac", "gp", "cc"):
            if ctx.cnt[s] > 0:
                p.wait(eng, s, ctx.cnt[s])
        for r in ctx.rings:
            for sl, (h, c) in enumerate(r.slots):
                if c > 0:
                    p.trace[eng].append(("w", f"{r.name}{sl}", c))
                    p.ops[eng].append(lambda e, h=h, v=c: e.wait_ge(h, v))


def build_nc(phases=ALL_PHASES, debug=(), attn_io=False):
    nc = bass.Bass("TRN2", target_bir_lowering=False, debug=False)
    ctx = Ctx(nc)
    d = ctx.dram
    qkv_kind = "ExternalInput" if attn_io else "Internal"

    d["x_fm"] = nc.dram_tensor("x_fm", [DM, TPC], F32, kind="ExternalInput").ap()
    d["gate_blk"] = nc.dram_tensor("gate_blk", [IC, 128, DC * 128], I8, kind="ExternalInput").ap()
    d["up_blk"] = nc.dram_tensor("up_blk", [IC, 128, DC * 128], I8, kind="ExternalInput").ap()
    d["down_blk"] = nc.dram_tensor("down_blk", [DC, 128, IC * 128], I8, kind="ExternalInput").ap()
    d["qkv_blk"] = nc.dram_tensor("qkv_blk", [48, 128, DC * 128], F32, kind="ExternalInput").ap()
    d["wo_blk"] = nc.dram_tensor("wo_blk", [DC, 128, DC * 128], F32, kind="ExternalInput").ap()
    d["qn_bc"] = nc.dram_tensor("qn_bc", [128, 1], F32, kind="ExternalInput").ap()
    d["kn_bc"] = nc.dram_tensor("kn_bc", [128, 1], F32, kind="ExternalInput").ap()
    d["scales"] = nc.dram_tensor("scales", [128, 3], F32, kind="ExternalInput").ap()
    d["tmpl"] = nc.dram_tensor("tmpl", [NJ, 128, 512], F32, kind="ExternalInput").ap()

    d["x1_fm"] = nc.dram_tensor("x1_fm", [DM, TPC], F32, kind="Internal").ap()
    d["x2_fm"] = nc.dram_tensor("x2_fm", [DM, TPC], F32, kind="Internal").ap()
    d["xn2_fm"] = nc.dram_tensor("xn2_fm", [DM, TPC], BF16, kind="Internal").ap()
    d["xn3_fm"] = nc.dram_tensor("xn3_fm", [DM, TPC], BF16, kind="Internal").ap()
    d["qf"] = nc.dram_tensor("qf", [NJ, H * 128, 128], BF16, kind=qkv_kind).ap()
    d["k_in"] = nc.dram_tensor("k_in", [HKV * 128, TPC], BF16, kind=qkv_kind).ap()
    d["v_in"] = nc.dram_tensor("v_in", [HKV * TPC, 128], BF16, kind=qkv_kind).ap()
    d["k_g"] = nc.dram_tensor("k_g", [4 * HKV * 128, TPC], BF16, kind="Internal").ap()
    d["v_g"] = nc.dram_tensor("v_g", [4 * HKV * TPC, 128], BF16, kind="Internal").ap()
    d["attn"] = nc.dram_tensor("attn", [H * 128, TPC], BF16,
                               kind="ExternalOutput" if attn_io else "Internal").ap()
    d["out"] = nc.dram_tensor("out", [DM, TPC], F32, kind="ExternalOutput").ap()

    if ATT_DEBUG:
        for nm_, sh_, dt2 in (("att_ssp", [128, 1024], F32), ("att_psn", [128, 1024], F32),
                              ("att_ptp", [128, 1024], F32), ("att_rows", [128, 16], F32)):
            d[nm_] = nc.dram_tensor(nm_, sh_, dt2, kind="ExternalOutput").ap()
    dbg_specs = {
        "x1": ([DM, TPC], F32, "x1_fm"), "xn2": ([DM, TPC], F32, "xn2_fm"),
        "qf": ([NJ, H * 128, 128], F32, "qf"), "k_in": ([HKV * 128, TPC], F32, "k_in"),
        "v_in": ([HKV * TPC, 128], F32, "v_in"), "k_g": ([4 * HKV * 128, TPC], F32, "k_g"),
        "attn": ([H * 128, TPC], F32, "attn"), "x2": ([DM, TPC], F32, "x2_fm"),
        "xn3": ([DM, TPC], F32, "xn3_fm"),
    }
    for name in debug:
        sh, dt_, _ = dbg_specs[name]
        d["dbg_" + name] = nc.dram_tensor("dbg_" + name, sh, dt_, kind="ExternalOutput").ap()

    with contextlib.ExitStack() as _stk:
        for s in ("ld", "wq", "st", "pe", "dv", "ac", "gp", "cc"):
            ctx.sem[s] = _stk.enter_context(nc.semaphore("s_" + s))
            ctx.cnt[s] = 0
        sbufs = dict(
            ones_col=([128, 1], F32), ones_row=([1, 128], F32),
            qn_sb=([128, 1], F32), kn_sb=([128, 1], F32), scl_sb=([128, 3], F32),
            hds_sb=([128, 1], F32), ident=([128, 128], BF16), ones_bf=([128, 128], BF16),
            eps_ln=([1, 1], F32), eps_rms=([128, 1], F32),
        )
        for nm_, (sh_, dt_) in sbufs.items():
            ctx.t[nm_] = _stk.enter_context(nc.sbuf_tensor(nm_, sh_, dt_))
        ctx.r["xc"] = Ring(ctx, _stk, "rxc", 2)
        ctx.r["xr"] = Ring(ctx, _stk, "rxr", 2)
        ctx.r["wg"] = Ring(ctx, _stk, "rwg", 2)
        ctx.r["wd"] = Ring(ctx, _stk, "rwd", 2)
        ctx.r["so"] = Ring(ctx, _stk, "rso", 2)
        ctx.r["sx"] = Ring(ctx, _stk, "rsx", 2)
        ctx.r["wq2"] = Ring(ctx, _stk, "rwq", 2)
        ctx.r["sq"] = Ring(ctx, _stk, "rsq", 2)
        ctx.r["sv"] = Ring(ctx, _stk, "rsv", 2)
        ctx.r["kv"] = Ring(ctx, _stk, "rkv", 16)
        ctx.r["lq"] = Ring(ctx, _stk, "rlq", 2)
        ctx.r["sa"] = Ring(ctx, _stk, "rsa", 2)
        ctx.r["la"] = Ring(ctx, _stk, "rla", 2)

        emit_setup(ctx)
        if "ffn1" in phases:
            for st in range(NST):
                emit_ffn_supertile(ctx, st, xin=d["x_fm"], xres=d["x_fm"], xout=d["x1_fm"],
                                   ln_in=True, xn_next=d["xn2_fm"], tag=f"a{st}")
        if "qkv" in phases:
            emit_qkv(ctx)
        if "gather" in phases:
            emit_gather(ctx)
        if "attn" in phases:
            emit_attention(ctx)
        if "oproj" in phases:
            for st in range(NST):
                emit_oproj_supertile(ctx, st)
        if "ffn2" in phases:
            for st in range(NST):
                emit_ffn_supertile(ctx, st, xin=d["xn3_fm"], xres=d["x2_fm"], xout=d["out"],
                                   ln_in=False, xn_next=None, tag=f"b{st}")
        emit_tail(ctx, debug)
    ctx.check_deadlock()
    return nc


def emit_setup(ctx):
    t, d = ctx.t, ctx.dram
    p = Prog(ctx)
    ld1 = op(p, ctx, "sync", lambda e: e.dma_start(out=t["qn_sb"][:], in_=d["qn_bc"][:]), "ld", 16)
    ld2 = op(p, ctx, "sync", lambda e: e.dma_start(out=t["kn_sb"][:], in_=d["kn_bc"][:]), "ld", 16)
    ld3 = op(p, ctx, "sync", lambda e: e.dma_start(out=t["scl_sb"][:], in_=d["scales"][:]), "ld", 16)
    op(p, ctx, "gpsimd", lambda e: e.memset(t["eps_ln"][:], LN_EPS))
    op(p, ctx, "gpsimd", lambda e: e.memset(t["eps_rms"][:], RMS_EPS))
    op(p, ctx, "gpsimd", lambda e: e.memset(t["ones_col"][:], 1.0))
    op(p, ctx, "gpsimd", lambda e: e.memset(t["ones_row"][:], 1.0))
    gp_ones = op(p, ctx, "gpsimd", lambda e: e.memset(t["ones_bf"][:], 1.0), "gp")
    p.wait("gpsimd", "gp", gp_ones)
    op(p, ctx, "gpsimd", lambda e: e.affine_select(t["ident"][:], t["ones_bf"][:], pattern=[[1, 128]],
                                                   compare_op=ALU.is_equal, fill=0.0, base=0,
                                                   channel_multiplier=-1), "gp")
    p.wait("vector", "ld", ld3)
    op(p, ctx, "vector", lambda e: e.tensor_scalar_mul(t["hds_sb"][:], t["scl_sb"][:, 2:3], 0.5), "dv")
    op(p, ctx, "vector", lambda e: e.tensor_scalar_mul(t["qn_sb"][:], t["qn_sb"][:],
                                                       1.0 / math.sqrt(HD)), "dv")
    p.emit()


def emit_ln_finalize(p, ctx, rows, ps_s1, ps_s2, ps_bc, m_bc, r_bc, pe_stats_done):
    t = ctx.t
    p.wait("vector", "pe", pe_stats_done)
    op(p, ctx, "vector", lambda e: e.tensor_scalar_mul(rows[0:1, 0:512], ps_s1[0:1, :], 1.0 / DM), "dv")
    dv_t = op(p, ctx, "vector", lambda e: e.tensor_scalar_mul(rows[0:1, 512:1024], ps_s2[0:1, :], 1.0 / DM), "dv")
    p.wait("vector", "dv", dv_t)
    dv_t = op(p, ctx, "vector", lambda e: e.tensor_mul(rows[0:1, 1024:1536], rows[0:1, 0:512],
                                                       rows[0:1, 0:512]), "dv")
    p.wait("vector", "dv", dv_t)
    dv_var = op(p, ctx, "vector", lambda e: e.tensor_sub(rows[0:1, 1536:2048], rows[0:1, 512:1024],
                                                         rows[0:1, 1024:1536]), "dv")
    p.wait("scalar", "dv", dv_var)
    ac_sd = op(p, ctx, "scalar", lambda e: e.activation(rows[0:1, 1024:1536], rows[0:1, 1536:2048],
                                                        AF.Sqrt, bias=t["eps_ln"][0:1, 0:1]), "ac")
    p.wait("vector", "ac", ac_sd)
    dv_rstd = op(p, ctx, "vector", lambda e: e.reciprocal(rows[0:1, 512:1024],
                                                          rows[0:1, 1024:1536]), "dv")
    p.wait("tensor", "dv", dv_rstd)
    op(p, ctx, "tensor", lambda e: e.matmul(ps_bc[:, 0:512], t["ones_row"][0:1, :],
                                            rows[0:1, 0:512], start=True, stop=True), "pe")
    pe_bc = op(p, ctx, "tensor", lambda e: e.matmul(ps_bc[:, 512:1024], t["ones_row"][0:1, :],
                                                    rows[0:1, 512:1024], start=True, stop=True), "pe")
    p.wait("vector", "pe", pe_bc)
    op(p, ctx, "vector", lambda e: e.tensor_copy(m_bc[:], ps_bc[:, 0:512]), "dv")
    dv_bc = op(p, ctx, "vector", lambda e: e.tensor_copy(r_bc[:], ps_bc[:, 512:1024]), "dv")
    return dv_bc


def emit_ln_normalize(p, ctx, xsrc, c0, xn_write, xc, tmp, m_bc, r_bc, dv_bc,
                      src_waiters=()):
    rxc, rsx = ctx.r["xc"], ctx.r["sx"]
    dv_n = {}
    st_n = {}
    for dc in range(DC):
        sl = (dc % 2) * 512
        if dc == 0:
            for w in src_waiters:
                w(p, "sync")
        if dc >= 2:
            p.wait("sync", "dv", dv_n[dc - 2])
        tk_x = rxc.issue(p, "sync", lambda e, dc=dc, sl=sl: e.dma_start(
            out=xc[:, sl:sl + 512], in_=xsrc[dc * 128:(dc + 1) * 128, c0:c0 + STW]))
        dest, sb_out = xn_write(dc)
        rxc.wait(p, "vector", tk_x)
        if dc == 0:
            p.wait("vector", "dv", dv_bc)
        if dest is not None and dc >= 2:
            rsx.wait(p, "vector", st_n[dc - 2])
        dv_t = op(p, ctx, "vector", lambda e, sl=sl: e.tensor_sub(tmp[:, sl:sl + 512],
                                                                  xc[:, sl:sl + 512], m_bc[:]), "dv")
        p.wait("vector", "dv", dv_t)
        dv_n[dc] = op(p, ctx, "vector", lambda e, sb_out=sb_out, sl=sl: e.tensor_mul(
            sb_out, tmp[:, sl:sl + 512], r_bc[:]), "dv")
        if dest is not None:
            p.wait("sync", "dv", dv_n[dc])
            st_n[dc] = rsx.issue(p, "sync", lambda e, dest=dest, sb_out=sb_out: e.dma_start(
                out=dest, in_=sb_out))


def emit_ffn_supertile(ctx, st, xin, xres, xout, ln_in, xn_next, tag):
    nc, t, d = ctx.nc, ctx.t, ctx.dram
    c0 = st * STW

    with (
        nc.sbuf_tensor(f"xn_{tag}", [128, DC * STW], BF16) as xn_fm,
        nc.sbuf_tensor(f"h_{tag}", [128, IC * STW], BF16) as h_fm,
    ):
        if ln_in:
            with (
                nc.sbuf_tensor(f"xa_{tag}", [128, 2 * 512], F32) as xc,
                nc.sbuf_tensor(f"sq_{tag}", [128, 2 * 512], F32) as sq,
                nc.sbuf_tensor(f"mb_{tag}", [128, 512], F32) as m_bc,
                nc.sbuf_tensor(f"rb_{tag}", [128, 512], F32) as r_bc,
                nc.sbuf_tensor(f"rw_{tag}", [1, 2048], F32) as rows,
                nc.psum_tensor(f"p1_{tag}", [128, 512], F32) as ps_s1,
                nc.psum_tensor(f"p2_{tag}", [128, 512], F32) as ps_s2,
                nc.psum_tensor(f"pb_{tag}", [128, 1024], F32) as ps_bc,
            ):
                p = Prog(ctx)
                barrier(p, ctx)
                rxc = ctx.r["xc"]
                ld_x = {}
                ac_sq = {}
                pe_mm = {}
                for dc in range(DC):
                    sl = (dc % 2) * 512
                    if dc >= 2:
                        p.wait("sync", "pe", pe_mm[dc - 2])
                    ld_x[dc] = rxc.issue(p, "sync", lambda e, dc=dc, sl=sl: e.dma_start(
                        out=xc[:, sl:sl + 512], in_=xin[dc * 128:(dc + 1) * 128, c0:c0 + STW]))
                    rxc.wait(p, "scalar", ld_x[dc])
                    if dc >= 2:
                        p.wait("scalar", "pe", pe_mm[dc - 2])
                    ac_sq[dc] = op(p, ctx, "scalar", lambda e, sl=sl: e.activation(
                        sq[:, sl:sl + 512], xc[:, sl:sl + 512], AF.Square), "ac")
                    p.wait("tensor", "ac", ac_sq[dc])
                    op(p, ctx, "tensor", lambda e, sl=sl, dc=dc: e.matmul(
                        ps_s1[0:1, :], t["ones_col"][:], xc[:, sl:sl + 512],
                        start=(dc == 0), stop=(dc == DC - 1)))
                    pe_mm[dc] = op(p, ctx, "tensor", lambda e, sl=sl, dc=dc: e.matmul(
                        ps_s2[0:1, :], t["ones_col"][:], sq[:, sl:sl + 512],
                        start=(dc == 0), stop=(dc == DC - 1)), "pe")
                dv_bc = emit_ln_finalize(p, ctx, rows, ps_s1, ps_s2, ps_bc, m_bc, r_bc,
                                         pe_stats_done=pe_mm[DC - 1])

                def xn_write(dc):
                    return None, xn_fm[:, dc * STW:(dc + 1) * STW]
                emit_ln_normalize(p, ctx, xin, c0, xn_write, xc, sq, m_bc, r_bc, dv_bc,
                                  src_waiters=(sem_waiter("pe", pe_mm[DC - 1]),))
                p.emit()
            xn_ready = ("dv", ctx.val("dv"))
        else:
            p = Prog(ctx)
            barrier(p, ctx)
            last = None
            for dc in range(DC):
                last = op(p, ctx, "sync", lambda e, dc=dc: e.dma_start(
                    out=xn_fm[:, dc * STW:(dc + 1) * STW],
                    in_=xin[dc * 128:(dc + 1) * 128, c0:c0 + STW]), "ld", 16)
            p.emit()
            xn_ready = ("ld", last)

        with (
            nc.sbuf_tensor(f"gw_{tag}", [128, 2 * DC * 128], BF16) as gw,
            nc.sbuf_tensor(f"uw_{tag}", [128, 2 * DC * 128], BF16) as uw,
            nc.sbuf_tensor(f"si_{tag}", [128, 512], F32) as silu_sb,
            nc.sbuf_tensor(f"us_{tag}", [128, 512], F32) as up_sb,
            nc.psum_tensor(f"pg_{tag}", [128, 1024], F32) as ps_g,
            nc.psum_tensor(f"pu_{tag}", [128, 1024], F32) as ps_u,
        ):
            p = Prog(ctx)
            wq_w = {}
            pe_u = {}
            ac_su = {}
            dv_h = {}
            p.wait("tensor", xn_ready[0], xn_ready[1])
            p.wait("gpsimd", xn_ready[0], xn_ready[1])
            for ic in range(IC):
                sl = (ic % 2) * DC * 128
                pb = (ic % 2) * 512
                if ic >= 2:
                    p.wait("gpsimd", "pe", pe_u[ic - 2])
                rwg = ctx.r["wg"]
                rwg.issue(p, "gpsimd", lambda e, ic=ic, sl=sl: e.dma_start(
                    out=gw[:, sl:sl + DC * 128], in_=d["gate_blk"][ic]), advance=False)
                wq_w[ic] = rwg.issue(p, "gpsimd", lambda e, ic=ic, sl=sl: e.dma_start(
                    out=uw[:, sl:sl + DC * 128], in_=d["up_blk"][ic]))

                rwg.wait(p, "tensor", wq_w[ic])
                if ic >= 2:
                    p.wait("tensor", "ac", ac_su[ic - 2])
                for dc in range(DC):
                    last = dc == DC - 1
                    op(p, ctx, "tensor", lambda e, sl=sl, pb=pb, dc=dc, last=last: e.matmul(
                        ps_g[:, pb:pb + 512], gw[:, sl + dc * 128: sl + dc * 128 + 128],
                        xn_fm[:, dc * STW:(dc + 1) * STW], start=(dc == 0), stop=last),
                        "pe" if last else None)
                    pe_u[ic] = op(p, ctx, "tensor", lambda e, sl=sl, pb=pb, dc=dc, last=last: e.matmul(
                        ps_u[:, pb:pb + 512], uw[:, sl + dc * 128: sl + dc * 128 + 128],
                        xn_fm[:, dc * STW:(dc + 1) * STW], start=(dc == 0), stop=last),
                        "pe" if last else None)

                p.wait("scalar", "pe", pe_u[ic])
                if ic >= 1:
                    p.wait("scalar", "dv", dv_h[ic - 1])
                op(p, ctx, "scalar", lambda e, pb=pb: e.activation(
                    silu_sb[:], ps_g[:, pb:pb + 512], AF.Silu, scale=t["scl_sb"][:, 0:1]), "ac")
                ac_su[ic] = op(p, ctx, "scalar", lambda e, pb=pb: e.activation(
                    up_sb[:], ps_u[:, pb:pb + 512], AF.Copy, scale=t["scl_sb"][:, 1:2]), "ac")

                p.wait("vector", "ac", ac_su[ic])
                dv_h[ic] = op(p, ctx, "vector", lambda e, ic=ic: e.tensor_mul(
                    h_fm[:, ic * STW:(ic + 1) * STW], silu_sb[:], up_sb[:]), "dv")
            p.emit()
        h_done = ctx.val("dv")

        do_ln = xn_next is not None
        with (
            nc.sbuf_tensor(f"dw_{tag}", [128, 2 * 43 * 128], BF16) as dw,
            nc.sbuf_tensor(f"xr_{tag}", [128, 2 * 512], F32) as xr,
            nc.sbuf_tensor(f"oc_{tag}", [128, 2 * 512], F32) as oc,
            nc.sbuf_tensor(f"sq2_{tag}", [128, 2 * 512], F32) as sqc,
            nc.sbuf_tensor(f"mb2_{tag}", [128, 512], F32) as m_bc2,
            nc.sbuf_tensor(f"rb2_{tag}", [128, 512], F32) as r_bc2,
            nc.sbuf_tensor(f"rw2_{tag}", [1, 2048], F32) as rows2,
            nc.sbuf_tensor(f"xnc_{tag}", [128, 2 * 512], F32) as xnc,
            nc.sbuf_tensor(f"xno_{tag}", [128, 2 * 512], BF16) as xno,
            nc.psum_tensor(f"pd_{tag}", [128, 1024], F32) as ps_d,
            nc.psum_tensor(f"p1b_{tag}", [128, 512], F32) as ps_s1,
            nc.psum_tensor(f"p2b_{tag}", [128, 512], F32) as ps_s2,
            nc.psum_tensor(f"pbb_{tag}", [128, 1024], F32) as ps_bc,
        ):
            p = Prog(ctx)
            p.wait("tensor", "dv", h_done)
            p.wait("gpsimd", "dv", h_done)
            p.wait("sync", "dv", h_done)
            wq_h = {}
            pe_half = {}
            pe_stats = {}
            dv_oc = {}
            ac_sq = {}
            st_oc = {}
            ld_xr = {}

            def emit_stats(sdc, start, stop):
                spb = (sdc % 2) * 512
                p.wait("tensor", "ac", ac_sq[sdc])
                op(p, ctx, "tensor", lambda e, spb=spb: e.matmul(
                    ps_s1[0:1, :], t["ones_col"][:], oc[:, spb:spb + 512],
                    start=start, stop=stop))
                pe_stats[sdc] = op(p, ctx, "tensor", lambda e, spb=spb: e.matmul(
                    ps_s2[0:1, :], t["ones_col"][:], sqc[:, spb:spb + 512],
                    start=start, stop=stop), "pe")

            for dc in range(DC):
                pb = (dc % 2) * 512
                for hf in range(2):
                    k = dc * 2 + hf
                    sl = (k % 2) * 43 * 128
                    if k >= 2:
                        p.wait("gpsimd", "pe", pe_half[k - 2])
                    wq_h[k] = ctx.r["wd"].issue(p, "gpsimd", lambda e, dc=dc, hf=hf, sl=sl: e.dma_start(
                        out=dw[:, sl:sl + 43 * 128],
                        in_=d["down_blk"][dc, :, hf * 43 * 128:(hf + 1) * 43 * 128]))

                if dc >= 2:
                    p.wait("sync", "dv", dv_oc[dc - 2])
                ld_xr[dc] = ctx.r["xr"].issue(p, "sync", lambda e, dc=dc, pb=pb: e.dma_start(
                    out=xr[:, pb:pb + 512], in_=xres[dc * 128:(dc + 1) * 128, c0:c0 + STW]))

                if do_ln and dc >= 2:
                    emit_stats(dc - 2, start=(dc - 2 == 0), stop=False)

                if dc >= 2:
                    p.wait("tensor", "dv", dv_oc[dc - 2])
                for hf in range(2):
                    k = dc * 2 + hf
                    sl = (k % 2) * 43 * 128
                    ctx.r["wd"].wait(p, "tensor", wq_h[k])
                    for i in range(43):
                        ic = hf * 43 + i
                        pe_half[k] = op(p, ctx, "tensor", lambda e, sl=sl, pb=pb, i=i, ic=ic: e.matmul(
                            ps_d[:, pb:pb + 512], dw[:, sl + i * 128: sl + i * 128 + 128],
                            h_fm[:, ic * STW:(ic + 1) * STW], start=(ic == 0), stop=(ic == IC - 1)),
                            "pe" if i == 42 else None)

                p.wait("vector", "pe", pe_half[dc * 2 + 1])
                ctx.r["xr"].wait(p, "vector", ld_xr[dc])
                if dc >= 2:
                    ctx.r["so"].wait(p, "vector", st_oc[dc - 2])
                    if do_ln:
                        p.wait("vector", "pe", pe_stats[dc - 2])
                dv_oc[dc] = op(p, ctx, "vector", lambda e, pb=pb: e.scalar_tensor_tensor(
                    oc[:, pb:pb + 512], ps_d[:, pb:pb + 512], t["hds_sb"][:, 0:1],
                    xr[:, pb:pb + 512], ALU.mult, ALU.add), "dv")

                if do_ln:
                    p.wait("scalar", "dv", dv_oc[dc])
                    if dc >= 2:
                        p.wait("scalar", "pe", pe_stats[dc - 2])
                    ac_sq[dc] = op(p, ctx, "scalar", lambda e, pb=pb: e.activation(
                        sqc[:, pb:pb + 512], oc[:, pb:pb + 512], AF.Square), "ac")

                p.wait("sync", "dv", dv_oc[dc])
                st_oc[dc] = ctx.r["so"].issue(p, "sync", lambda e, dc=dc, pb=pb: e.dma_start(
                    out=xout[dc * 128:(dc + 1) * 128, c0:c0 + STW], in_=oc[:, pb:pb + 512]))

            if do_ln:
                emit_stats(DC - 2, start=False, stop=False)
                emit_stats(DC - 1, start=False, stop=True)
                dv_bc = emit_ln_finalize(p, ctx, rows2, ps_s1, ps_s2, ps_bc, m_bc2, r_bc2,
                                         pe_stats_done=pe_stats[DC - 1])

                def xn_write(dc):
                    sl = (dc % 2) * 512
                    return (xn_next[dc * 128:(dc + 1) * 128, c0:c0 + STW], xno[:, sl:sl + 512])
                emit_ln_normalize(p, ctx, xout, c0, xn_write, xnc, sqc, m_bc2, r_bc2, dv_bc,
                                  src_waiters=(ctx.r["so"].waiter(st_oc[DC - 2]),
                                               ctx.r["so"].waiter(st_oc[DC - 1]),
                                               sem_waiter("pe", pe_stats[DC - 1])))
            p.emit()


def emit_qkv(ctx):
    nc, t, d = ctx.nc, ctx.t, ctx.dram
    with (
        nc.sbuf_tensor("xn2_sb", [128, DC * TPC], BF16) as xn2_sb,
        nc.sbuf_tensor("qkvw", [128, 2 * DC * 128], BF16) as wsb,
        nc.sbuf_tensor("sqq", [128, 2 * 512], F32) as sqq,
        nc.sbuf_tensor("rmsr", [1, 4 * 512], F32) as rmsr,
        nc.sbuf_tensor("rbcq", [128, 2 * 512], F32) as rbc,
        nc.sbuf_tensor("qbf", [128, 2 * 512], BF16) as qbf,
        nc.sbuf_tensor("vtr", [128, 2 * 512], BF16) as vtr,
        nc.psum_tensor("ps_qkv", [128, 4 * 512], F32) as ps_q,
        nc.psum_tensor("ps_rs", [128, 512], F32) as ps_rs,
        nc.psum_tensor("ps_rb", [128, 512], F32) as ps_rb,
        nc.psum_tensor("ps_vt", [128, 2048], BF16) as ps_vt,
    ):
        p = Prog(ctx)
        barrier(p, ctx)
        ld_xn = None
        for dc in range(DC):
            ld_xn = op(p, ctx, "sync", lambda e, dc=dc: e.dma_start(
                out=xn2_sb[:, dc * TPC:(dc + 1) * TPC],
                in_=d["xn2_fm"][dc * 128:(dc + 1) * 128, :]), "ld", 16)
        p.wait("tensor", "ld", ld_xn)

        wq_c = {}
        pe_mm = {}
        ac_sq = {}
        pe_s2 = {}
        dv_row = {}
        dv_rstd = {}
        pe_bc = {}
        srb_free = {}
        psum_free = {}
        qbf_free = {}
        dv_cp_v = {}
        pe_tr = {}
        pvt_free = {}
        vtr_free = {}
        NQS = 4

        def stage1(k):
            cc, th = divmod(k, 2)
            pb = (k % NQS) * 512
            sl2 = (k % 2) * 512
            if cc < 40:
                p.wait("scalar", "pe", pe_mm[k])
                if k >= 2 and (k - 2) in pe_s2:
                    p.wait("scalar", "pe", pe_s2[k - 2])
                ac_sq[k] = op(p, ctx, "scalar", lambda e, pb=pb, sl2=sl2: e.activation(
                    sqq[:, sl2:sl2 + 512], ps_q[:, pb:pb + 512], AF.Square), "ac")
            else:
                p.wait("vector", "pe", pe_mm[k])
                if k >= 2:
                    qbf_free[k - 2](p, "vector")
                dv_cp_v[k] = op(p, ctx, "vector", lambda e, pb=pb, sl2=sl2: e.tensor_copy(
                    qbf[:, sl2:sl2 + 512], ps_q[:, pb:pb + 512]), "dv")
                psum_free[k] = sem_waiter("dv", dv_cp_v[k])

        def stage2(k):
            cc, th = divmod(k, 2)
            sl2 = (k % 2) * 512
            if cc < 40:
                p.wait("tensor", "ac", ac_sq[k])
                if k >= 1 and (k - 1) in dv_row:
                    p.wait("tensor", "dv", dv_row[k - 1])
                pe_s2[k] = op(p, ctx, "tensor", lambda e, sl2=sl2: e.matmul(
                    ps_rs[0:1, :], t["ones_col"][:], sqq[:, sl2:sl2 + 512],
                    start=True, stop=True), "pe")
                p.wait("vector", "pe", pe_s2[k])
                if k >= 2 and (k - 2) in pe_bc:
                    p.wait("vector", "pe", pe_bc[k - 2])
                dv_row[k] = op(p, ctx, "vector", lambda e, sl2=sl2: e.tensor_scalar_mul(
                    rmsr[0:1, sl2:sl2 + 512], ps_rs[0:1, :], 1.0 / HD), "dv")
                p.wait("scalar", "dv", dv_row[k])
                ac_sd = op(p, ctx, "scalar", lambda e, sl2=sl2: e.activation(
                    rmsr[0:1, 1024 + sl2:1024 + sl2 + 512], rmsr[0:1, sl2:sl2 + 512], AF.Sqrt,
                    bias=t["eps_rms"][0:1, 0:1]), "ac")
                p.wait("vector", "ac", ac_sd)
                dv_rstd[k] = op(p, ctx, "vector", lambda e, sl2=sl2: e.reciprocal(
                    rmsr[0:1, sl2:sl2 + 512], rmsr[0:1, 1024 + sl2:1024 + sl2 + 512]), "dv")
            else:
                p.wait("tensor", "dv", dv_cp_v[k])
                if k >= 2 and (k - 2) in pvt_free:
                    pvt_free[k - 2](p, "tensor")
                pt = None
                for b_ in range(4):
                    pt = op(p, ctx, "tensor", lambda e, sl2=sl2, b_=b_, k=k: e.transpose(
                        ps_vt[:, (k % 2) * 1024 + b_ * 128:(k % 2) * 1024 + b_ * 128 + 128],
                        qbf[:, sl2 + b_ * 128: sl2 + b_ * 128 + 128], t["ident"][:]),
                        "pe" if b_ == 3 else None)
                pe_tr[k] = pt
                qbf_free[k] = sem_waiter("pe", pt)
                p.wait("vector", "pe", pt)
                if k >= 2 and (k - 2) in vtr_free:
                    vtr_free[k - 2](p, "vector")
                dv_ev = op(p, ctx, "vector", lambda e, sl2=sl2, k=k: e.tensor_copy(
                    vtr[:, sl2:sl2 + 512], ps_vt[:, (k % 2) * 1024:(k % 2) * 1024 + 512]), "dv")
                pvt_free[k] = sem_waiter("dv", dv_ev)
                p.wait("sync", "dv", dv_ev)
                kv = cc - 40
                dst = d["v_in"][kv * TPC + th * 512: kv * TPC + (th + 1) * 512, :].rearrange(
                    "(b p) hd -> p b hd", p=128)
                st_o = ctx.r["sv"].issue(p, "sync", lambda e, dst=dst, sl2=sl2: e.dma_start(
                    out=dst, in_=vtr[:, sl2:sl2 + 512]))
                vtr_free[k] = ctx.r["sv"].waiter(st_o)

        def stage3(k):
            cc, th = divmod(k, 2)
            if cc >= 40:
                return
            pb = (k % NQS) * 512
            sl2 = (k % 2) * 512
            p.wait("tensor", "dv", dv_rstd[k])
            if k >= 1 and (k - 1) in srb_free:
                srb_free[k - 1](p, "tensor")
            pe_bc[k] = op(p, ctx, "tensor", lambda e, sl2=sl2: e.matmul(
                ps_rb[:, :], t["ones_row"][0:1, :], rmsr[0:1, sl2:sl2 + 512],
                start=True, stop=True), "pe")
            p.wait("vector", "pe", pe_bc[k])
            dv_cp = op(p, ctx, "vector", lambda e, sl2=sl2: e.tensor_copy(
                rbc[:, sl2:sl2 + 512], ps_rb[:, :]), "dv")
            srb_free[k] = sem_waiter("dv", dv_cp)
            p.wait("vector", "dv", dv_cp)
            if k >= 2:
                qbf_free[k - 2](p, "vector")
            w_ap = t["qn_sb"] if cc < 32 else t["kn_sb"]
            dv_stt = op(p, ctx, "vector", lambda e, pb=pb, sl2=sl2, w_ap=w_ap: e.scalar_tensor_tensor(
                qbf[:, sl2:sl2 + 512], ps_q[:, pb:pb + 512], w_ap[:, 0:1],
                rbc[:, sl2:sl2 + 512], ALU.mult, ALU.mult), "dv")
            psum_free[k] = sem_waiter("dv", dv_stt)
            p.wait("sync", "dv", dv_stt)
            if cc < 32:
                dst = d["qf"][:, cc * 128:(cc + 1) * 128, :].rearrange(
                    "j hd q -> hd j q")[:, th * 4:(th + 1) * 4, :]
            else:
                kv = cc - 32
                dst = d["k_in"][kv * 128:(kv + 1) * 128, th * 512:(th + 1) * 512]
            st_o = ctx.r["sq"].issue(p, "sync", lambda e, dst=dst, sl2=sl2: e.dma_start(
                out=dst, in_=qbf[:, sl2:sl2 + 512]))
            qbf_free[k] = ctx.r["sq"].waiter(st_o)

        for cc in range(48):
            if cc >= 2:
                p.wait("gpsimd", "pe", pe_mm[(cc - 2) * 2 + 1])
            wq_c[cc] = ctx.r["wq2"].issue(p, "gpsimd", lambda e, cc=cc: e.dma_start(
                out=wsb[:, (cc % 2) * DC * 128:(cc % 2) * DC * 128 + DC * 128],
                in_=d["qkv_blk"][cc]))
            for th in range(2):
                k = cc * 2 + th
                pb = (k % NQS) * 512
                wsl = (cc % 2) * DC * 128
                ctx.r["wq2"].wait(p, "tensor", wq_c[cc])
                if k >= NQS:
                    psum_free[k - NQS](p, "tensor")
                for dc in range(DC):
                    pe_mm[k] = op(p, ctx, "tensor", lambda e, wsl=wsl, pb=pb, dc=dc, th=th: e.matmul(
                        ps_q[:, pb:pb + 512], wsb[:, wsl + dc * 128: wsl + dc * 128 + 128],
                        xn2_sb[:, dc * TPC + th * 512: dc * TPC + th * 512 + 512],
                        start=(dc == 0), stop=(dc == DC - 1)), "pe" if dc == DC - 1 else None)
                if k >= 1:
                    stage2(k - 1)
                if k >= 2:
                    stage3(k - 2)
                stage1(k)
        stage2(95)
        stage3(94)
        stage3(95)
        p.emit()


def emit_gather(ctx):
    nc, d = ctx.nc, ctx.dram
    p = Prog(ctx)
    barrier(p, ctx, engines=("gpsimd",))
    groups = [[0, 1, 2, 3], [4, 5, 6, 7]]
    cc1 = op(p, ctx, "gpsimd", lambda e: e.collective_compute(
        "AllGather", ALU.bypass, replica_groups=groups,
        ins=[d["k_in"]], outs=[d["k_g"]]), "cc")
    cc2 = op(p, ctx, "gpsimd", lambda e: e.collective_compute(
        "AllGather", ALU.bypass, replica_groups=groups,
        ins=[d["v_in"]], outs=[d["v_g"]]), "cc")
    p.wait("gpsimd", "cc", cc2)
    p.emit()


def emit_attention(ctx):
    nc, t, d = ctx.nc, ctx.t, ctx.dram
    with (
        nc.sbuf_tensor("tmpl_sb", [128, NJ * 512], F32) as tmpl_sb,
        nc.sbuf_tensor("qj_sb", [128, H * 128], BF16) as qj,
        nc.sbuf_tensor("ksp", [128, 2 * NJ * 512], BF16) as ksp,
        nc.sbuf_tensor("vsp", [128, 2 * NJ * 512], BF16) as vsp,
        nc.sbuf_tensor("ssp", [128, 2 * 4096], F32) as ssp,
        nc.sbuf_tensor("psp", [128, 2 * 4096], BF16) as psp,
        nc.sbuf_tensor("psn", [128, 2 * 4096], BF16) as psn,
        nc.sbuf_tensor("ptp", [128, NJ * 1024], BF16) as ptp,
        nc.sbuf_tensor("nmx", [128, 8], F32) as nmx,
        nc.sbuf_tensor("rsm", [128, 8], F32) as rsm,
        nc.sbuf_tensor("rcp", [128, 8], F32) as rcp,
        nc.sbuf_tensor("rse", [128, 2], F32) as rse,
        nc.sbuf_tensor("obf", [128, 2 * H * 128], BF16) as obf,
        nc.psum_tensor("ps_sc", [128, 2048], F32) as ps_sc,
        nc.psum_tensor("ps_pt", [128, 2048], BF16) as ps_pt,
        nc.psum_tensor("ps_o", [128, 1024], F32) as ps_o,
    ):
        p = Prog(ctx)
        barrier(p, ctx)
        ld_tm = op(p, ctx, "sync", lambda e: e.dma_start(
            out=tmpl_sb[:].rearrange("p (dd c) -> p dd c", dd=NJ),
            in_=d["tmpl"].rearrange("dd p c -> p dd c")), "ld", 16)
        p.wait("vector", "ld", ld_tm)

        G = dict(last_qk=None, last_pv=None, sc_guard={}, pt_guard={}, exp_guard={},
                 pn_guard={}, psn_guard={}, o_guard={}, obf_guard={}, nm_guard={},
                 rs_guard={}, rc_guard={}, pp=0, qk_par={}, pv_par={})

        def attn_pair(j, g, pr, ld_q, ld_k, ld_v, gb):
            span = (j + 1) * 512
            kb = gb * NJ * 512
            pp = G["pp"]
            h0 = g * 4 + 2 * pr
            dv_sc = {}
            pe_qk = {}
            for kc in range(j + 1):
                scb = (kc % 2) * 1024
                ctx.r["kv"].wait(p, "tensor", ld_k[kc])
                ctx.r["lq"].wait(p, "tensor", ld_q)
                if (kc % 2) in G["sc_guard"]:
                    p.wait("tensor", "dv", G["sc_guard"][kc % 2])
                for hh in range(2):
                    h = h0 + hh
                    pe_qk[kc] = op(p, ctx, "tensor", lambda e, scb=scb, h=h, hh=hh, kc=kc, kb=kb: e.matmul(
                        ps_sc[:, scb + hh * 512: scb + hh * 512 + 512],
                        qj[:, h * 128:(h + 1) * 128], ksp[:, kb + kc * 512: kb + (kc + 1) * 512],
                        start=True, stop=True), "pe" if hh == 1 else None)
                G["last_qk"] = G["qk_par"][gb] = pe_qk[kc]
                p.wait("vector", "pe", pe_qk[kc])
                if kc == 0:
                    for hh in range(2):
                        if hh in G["exp_guard"]:
                            p.wait("vector", "ac", G["exp_guard"][hh])
                for hh in range(2):
                    h = h0 + hh
                    dv_sc[kc] = op(p, ctx, "vector", lambda e, scb=scb, hh=hh, kc=kc, j=j, h=h: e.scalar_tensor_tensor(
                        ssp[:, hh * 4096 + kc * 512: hh * 4096 + kc * 512 + 512],
                        tmpl_sb[:, (j - kc) * 512:(j - kc + 1) * 512], float(SLOPES[h]),
                        ps_sc[:, scb + hh * 512: scb + hh * 512 + 512],
                        ALU.mult, ALU.add), "dv" if hh == 1 else None)
                G["sc_guard"][kc % 2] = dv_sc[kc]

            p.wait("vector", "dv", dv_sc[j])
            if pp % 2 in G["nm_guard"]:
                p.wait("vector", "ac", G["nm_guard"][pp % 2])
            dv_nm = {}
            for hh in range(2):
                col = (pp % 2) * 2 + hh
                dv_nm[hh] = op(p, ctx, "vector", lambda e, hh=hh, col=col, span=span: e.reduce_max(
                    nmx[:, col:col + 1], ssp[:, hh * 4096: hh * 4096 + span],
                    axis=AX.X, negate=True), "dv")
            for hh in range(2):
                if hh in G["pn_guard"]:
                    p.wait("scalar", "dv", G["pn_guard"][hh])
            if pp % 2 in G["rs_guard"]:
                p.wait("scalar", "dv", G["rs_guard"][pp % 2])
            ac_e2 = None
            for hh in range(2):
                col = (pp % 2) * 2 + hh
                p.wait("scalar", "dv", dv_nm[hh])
                ac_e2 = op(p, ctx, "scalar", lambda e, hh=hh, col=col, span=span: e.activation(
                    psp[:, hh * 4096: hh * 4096 + span], ssp[:, hh * 4096: hh * 4096 + span],
                    AF.Exp, bias=nmx[:, col:col + 1], accum_out=rsm[:, col:col + 1]), "ac")
            p.wait("scalar", "ac", ac_e2)
            ac_exp = op(p, ctx, "scalar", lambda e, pp=pp: e.activation(
                rse[:, 0:2], rsm[:, (pp % 2) * 2:(pp % 2) * 2 + 2], AF.Copy), "ac")
            G["exp_guard"][0] = G["exp_guard"][1] = ac_exp
            G["nm_guard"][pp % 2] = ac_exp
            p.wait("vector", "ac", ac_exp)
            if pp % 2 in G["rc_guard"]:
                p.wait("vector", "dv", G["rc_guard"][pp % 2])
            dv_rc = op(p, ctx, "vector", lambda e, pp=pp: e.reciprocal(
                rcp[:, (pp % 2) * 2:(pp % 2) * 2 + 2], rsm[:, (pp % 2) * 2:(pp % 2) * 2 + 2]), "dv")
            G["rs_guard"][pp % 2] = dv_rc
            p.wait("vector", "dv", dv_rc)
            dv_pn = {}
            for hh in range(2):
                col = (pp % 2) * 2 + hh
                if hh in G["psn_guard"]:
                    p.wait("vector", "pe", G["psn_guard"][hh])
                dv_pn[hh] = op(p, ctx, "vector", lambda e, hh=hh, col=col, span=span: e.tensor_scalar(
                    psn[:, hh * 4096: hh * 4096 + span], psp[:, hh * 4096: hh * 4096 + span],
                    rcp[:, col:col + 1], None, ALU.mult), "dv")
            G["rc_guard"][pp % 2] = dv_pn[1]

            dv_pt = {}
            pe_pt = {}
            for kc in range(j + 1):
                ptb = (kc % 2) * 1024
                if kc == 0:
                    p.wait("tensor", "dv", dv_pn[1])
                    if G["last_pv"] is not None:
                        p.wait("vector", "pe", G["last_pv"])
                if (kc % 2) in G["pt_guard"]:
                    p.wait("tensor", "ac", G["pt_guard"][kc % 2])
                pe_l = None
                for sub in range(4):
                    for hh in range(2):
                        pe_l = op(p, ctx, "tensor", lambda e, ptb=ptb, sub=sub, hh=hh, kc=kc: e.transpose(
                            ps_pt[:, ptb + (sub * 2 + hh) * 128: ptb + (sub * 2 + hh) * 128 + 128],
                            psn[:, hh * 4096 + kc * 512 + sub * 128: hh * 4096 + kc * 512 + sub * 128 + 128],
                            t["ident"][:]), "pe" if (sub == 3 and hh == 1) else None)
                pe_pt[kc] = pe_l
                p.wait("scalar", "pe", pe_pt[kc])
                dv_pt[kc] = op(p, ctx, "scalar", lambda e, ptb=ptb, kc=kc: e.copy(
                    ptp[:, kc * 1024:(kc + 1) * 1024], ps_pt[:, ptb:ptb + 1024]), "ac")
                G["pt_guard"][kc % 2] = dv_pt[kc]
            G["pn_guard"][0] = dv_pn[0]
            G["pn_guard"][1] = dv_pn[1]
            G["psn_guard"][0] = G["psn_guard"][1] = pe_pt[j]

            ob = (pp % 2) * 512
            if pp % 2 in G["o_guard"]:
                p.wait("tensor", "dv", G["o_guard"][pp % 2])
            pe_pv = None
            for kc in range(j + 1):
                p.wait("tensor", "ac", dv_pt[kc])
                ctx.r["kv"].wait(p, "tensor", ld_v[kc])
                for sub in range(4):
                    pe_pv = op(p, ctx, "tensor", lambda e, kc=kc, sub=sub, ob=ob, j=j, kb=kb: e.matmul(
                        ps_o[:, ob:ob + 256], vsp[:, kb + (kc * 4 + sub) * 128: kb + (kc * 4 + sub) * 128 + 128],
                        ptp[:, (kc * 4 + sub) * 256:(kc * 4 + sub) * 256 + 256],
                        start=(kc == 0 and sub == 0), stop=(kc == j and sub == 3)),
                        "pe" if (kc == j and sub == 3) else None)
            G["last_pv"] = G["pv_par"][gb] = pe_pv
            p.wait("vector", "pe", pe_pv)
            if j >= 2:
                ctx.r["sa"].wait(p, "vector", G["obf_guard"][j % 2])
            dv_oev = op(p, ctx, "vector", lambda e, j=j, g=g, pr=pr, ob=ob: e.tensor_copy(
                obf[:, (j % 2) * H * 128 + (g * 4 + 2 * pr) * 128:
                    (j % 2) * H * 128 + (g * 4 + 2 * pr) * 128 + 256],
                ps_o[:, ob:ob + 256]), "dv")
            G["o_guard"][pp % 2] = dv_oev
            G["pp"] = pp + 1
            return dv_oev

        for j in range(NJ):
            if j >= 2:
                ctx.r["sa"].wait(p, "sync", G["obf_guard"][j % 2])
            if G["last_qk"] is not None:
                p.wait("sync", "pe", G["last_qk"])
            ld_q = ctx.r["lq"].issue(p, "sync", lambda e, j=j: e.dma_start(
                out=qj[:].rearrange("p (h q) -> p h q", h=H),
                in_=d["qf"][j].rearrange("(h hd) q -> hd h q", hd=128)))
            dv_oev = None
            for g in range(8):
                gb = (j * 8 + g) % 2
                kb = gb * NJ * 512
                if gb in G["qk_par"]:
                    p.wait("gpsimd", "pe", G["qk_par"][gb])
                if gb in G["pv_par"]:
                    p.wait("gpsimd", "pe", G["pv_par"][gb])
                ld_k = {}
                ld_v = {}
                for kc in range(j + 1):
                    src_k = d["k_g"].rearrange("(r kv hd) tt -> r kv hd tt", r=4, kv=HKV)[
                        :, g, :, kc * 128:(kc + 1) * 128].rearrange("r hd tt -> hd r tt")
                    ctx.r["kv"].issue(p, "gpsimd", lambda e, kc=kc, src_k=src_k, kb=kb: e.dma_start(
                        out=ksp[:, kb + kc * 512: kb + (kc + 1) * 512], in_=src_k), advance=False)
                    src_v = d["v_g"].rearrange("(r kv tt) hd -> r kv tt hd", r=4, kv=HKV)[
                        :, g, kc * 128:(kc + 1) * 128, :].rearrange("r tt hd -> tt r hd")
                    tok = ctx.r["kv"].issue(p, "gpsimd", lambda e, kc=kc, src_v=src_v, kb=kb: e.dma_start(
                        out=vsp[:, kb + kc * 512: kb + (kc + 1) * 512], in_=src_v))
                    ld_k[kc] = tok
                    ld_v[kc] = tok
                for pr in range(2):
                    dv_oev = attn_pair(j, g, pr, ld_q, ld_k, ld_v, gb)

            p.wait("sync", "dv", dv_oev)
            dst = d["attn"][:, j * 128:(j + 1) * 128].rearrange("(h hd) q -> hd h q", hd=128)
            st_o = ctx.r["sa"].issue(p, "sync", lambda e, dst=dst, j=j: e.dma_start(
                out=dst, in_=obf[:, (j % 2) * H * 128:(j % 2 + 1) * H * 128].rearrange(
                    "p (h q) -> p h q", h=H)))
            G["obf_guard"][j % 2] = st_o
        p.emit()


def emit_oproj_supertile(ctx, st):
    nc, t, d = ctx.nc, ctx.t, ctx.dram
    c0 = st * STW
    tag = f"op{st}"
    with (
        nc.sbuf_tensor(f"asb_{tag}", [128, DC * 512], BF16) as attn_sb,
        nc.sbuf_tensor(f"wos_{tag}", [128, 2 * DC * 128], BF16) as wsb,
        nc.sbuf_tensor(f"x1r_{tag}", [128, 2 * 512], F32) as xr,
        nc.sbuf_tensor(f"oc3_{tag}", [128, 2 * 512], F32) as oc,
        nc.sbuf_tensor(f"sq3_{tag}", [128, 2 * 512], F32) as sqc,
        nc.sbuf_tensor(f"mb3_{tag}", [128, 512], F32) as m_bc,
        nc.sbuf_tensor(f"rb3_{tag}", [128, 512], F32) as r_bc,
        nc.sbuf_tensor(f"rw3_{tag}", [1, 2048], F32) as rows,
        nc.sbuf_tensor(f"xnc3_{tag}", [128, 2 * 512], F32) as xnc,
        nc.sbuf_tensor(f"xno3_{tag}", [128, 2 * 512], BF16) as xno,
        nc.psum_tensor(f"po_{tag}", [128, 1024], F32) as ps_o2,
        nc.psum_tensor(f"p1c_{tag}", [128, 512], F32) as ps_s1,
        nc.psum_tensor(f"p2c_{tag}", [128, 512], F32) as ps_s2,
        nc.psum_tensor(f"pbc_{tag}", [128, 1024], F32) as ps_bc,
    ):
        p = Prog(ctx)
        barrier(p, ctx)
        ld_a = None
        for cc in range(DC):
            ld_a = op(p, ctx, "sync", lambda e, cc=cc: e.dma_start(
                out=attn_sb[:, cc * 512:(cc + 1) * 512],
                in_=d["attn"][cc * 128:(cc + 1) * 128, c0:c0 + STW]), "ld", 16)
        p.wait("tensor", "ld", ld_a)

        wq_m = {}
        pe_mm = {}
        pe_stats = {}
        dv_oc = {}
        ac_sq = {}
        st_oc = {}
        ld_xr = {}

        def emit_stats(sdc, start, stop):
            spb = (sdc % 2) * 512
            p.wait("tensor", "ac", ac_sq[sdc])
            op(p, ctx, "tensor", lambda e, spb=spb: e.matmul(
                ps_s1[0:1, :], t["ones_col"][:], oc[:, spb:spb + 512],
                start=start, stop=stop))
            pe_stats[sdc] = op(p, ctx, "tensor", lambda e, spb=spb: e.matmul(
                ps_s2[0:1, :], t["ones_col"][:], sqc[:, spb:spb + 512],
                start=start, stop=stop), "pe")

        for mc in range(DC):
            pb = (mc % 2) * 512
            wsl = (mc % 2) * DC * 128
            if mc >= 2:
                p.wait("gpsimd", "pe", pe_mm[mc - 2])
            wq_m[mc] = ctx.r["wq2"].issue(p, "gpsimd", lambda e, mc=mc: e.dma_start(
                out=wsb[:, (mc % 2) * DC * 128:(mc % 2) * DC * 128 + DC * 128],
                in_=d["wo_blk"][mc]))

            if mc >= 2:
                p.wait("sync", "dv", dv_oc[mc - 2])
            ld_xr[mc] = ctx.r["xr"].issue(p, "sync", lambda e, mc=mc, pb=pb: e.dma_start(
                out=xr[:, pb:pb + 512], in_=d["x1_fm"][mc * 128:(mc + 1) * 128, c0:c0 + STW]))

            if mc >= 2:
                emit_stats(mc - 2, start=(mc - 2 == 0), stop=False)

            ctx.r["wq2"].wait(p, "tensor", wq_m[mc])
            if mc >= 2:
                p.wait("tensor", "dv", dv_oc[mc - 2])
            for cc in range(DC):
                pe_mm[mc] = op(p, ctx, "tensor", lambda e, wsl=wsl, pb=pb, cc=cc: e.matmul(
                    ps_o2[:, pb:pb + 512], wsb[:, wsl + cc * 128: wsl + cc * 128 + 128],
                    attn_sb[:, cc * 512:(cc + 1) * 512], start=(cc == 0), stop=(cc == DC - 1)),
                    "pe" if cc == DC - 1 else None)

            p.wait("vector", "pe", pe_mm[mc])
            ctx.r["xr"].wait(p, "vector", ld_xr[mc])
            if mc >= 2:
                ctx.r["so"].wait(p, "vector", st_oc[mc - 2])
                p.wait("vector", "pe", pe_stats[mc - 2])
            dv_oc[mc] = op(p, ctx, "vector", lambda e, pb=pb: e.tensor_add(
                oc[:, pb:pb + 512], ps_o2[:, pb:pb + 512], xr[:, pb:pb + 512]), "dv")

            p.wait("scalar", "dv", dv_oc[mc])
            if mc >= 2:
                p.wait("scalar", "pe", pe_stats[mc - 2])
            ac_sq[mc] = op(p, ctx, "scalar", lambda e, pb=pb: e.activation(
                sqc[:, pb:pb + 512], oc[:, pb:pb + 512], AF.Square), "ac")

            p.wait("sync", "dv", dv_oc[mc])
            st_oc[mc] = ctx.r["so"].issue(p, "sync", lambda e, mc=mc, pb=pb: e.dma_start(
                out=d["x2_fm"][mc * 128:(mc + 1) * 128, c0:c0 + STW], in_=oc[:, pb:pb + 512]))

        emit_stats(DC - 2, start=False, stop=False)
        emit_stats(DC - 1, start=False, stop=True)
        dv_bc = emit_ln_finalize(p, ctx, rows, ps_s1, ps_s2, ps_bc, m_bc, r_bc,
                                 pe_stats_done=pe_stats[DC - 1])

        def xn_write(dc):
            sl = (dc % 2) * 512
            return (d["xn3_fm"][dc * 128:(dc + 1) * 128, c0:c0 + STW], xno[:, sl:sl + 512])
        emit_ln_normalize(p, ctx, d["x2_fm"], c0, xn_write, xnc, sqc, m_bc, r_bc, dv_bc,
                          src_waiters=(ctx.r["so"].waiter(st_oc[DC - 2]),
                                       ctx.r["so"].waiter(st_oc[DC - 1]),
                                       sem_waiter("pe", pe_stats[DC - 1])))
        p.emit()


def emit_tail(ctx, debug):
    d = ctx.dram
    p = Prog(ctx)
    barrier(p, ctx, engines=("sync",))
    srcs = {"x1": "x1_fm", "xn2": "xn2_fm", "qf": "qf", "k_in": "k_in", "v_in": "v_in",
            "k_g": "k_g", "attn": "attn", "x2": "x2_fm", "xn3": "xn3_fm"}
    barrier(p, ctx, engines=("gpsimd",))
    for name in debug:
        op(p, ctx, "gpsimd", lambda e, a=d["dbg_" + name], b_=d[srcs[name]]: e.dma_start(
            out=a[:], in_=b_[:]), "st", 16)
    p.wait("sync", "st", ctx.val("st"))
    p.emit()



def _stage_shared(L0_attn_Wq, L0_attn_Wk, L0_attn_Wv, L0_attn_Wo,
                  L0_attn_q_norm_w, L0_attn_k_norm_w,
                  L0_ffn_gate_w, L0_ffn_gate_s, L0_ffn_up_w, L0_ffn_up_s,
                  L0_ffn_down_w, L0_ffn_down_s):
    gate_wT = np.ascontiguousarray(np.asarray(L0_ffn_gate_w).T)
    up_wT = np.ascontiguousarray(np.asarray(L0_ffn_up_w).T)
    down_wT = np.ascontiguousarray(np.asarray(L0_ffn_down_w).T)

    def blk_a(wT):
        return np.ascontiguousarray(
            wT.reshape(DC, 128, IC, 128).transpose(2, 1, 0, 3).reshape(IC, 128, DC * 128)
        ).astype(np.int8)

    gate_blk = blk_a(gate_wT)
    up_blk = blk_a(up_wT)
    down_blk = np.ascontiguousarray(
        down_wT.reshape(IC, 128, DC, 128).transpose(2, 1, 0, 3).reshape(DC, 128, IC * 128)
    ).astype(np.int8)

    qkv_wT = np.ascontiguousarray(np.concatenate(
        [np.asarray(L0_attn_Wq).T, np.asarray(L0_attn_Wk).T, np.asarray(L0_attn_Wv).T],
        axis=1)).astype(np.float32)
    qkv_blk = np.ascontiguousarray(
        qkv_wT.reshape(DC, 128, 48, 128).transpose(2, 1, 0, 3).reshape(48, 128, DC * 128)
    ).astype(np.float32)
    WoT = np.ascontiguousarray(np.asarray(L0_attn_Wo).T)
    wo_blk = np.ascontiguousarray(
        WoT.reshape(DC, 128, DC, 128).transpose(2, 1, 0, 3).reshape(DC, 128, DC * 128)
    ).astype(np.float32)

    qn_bc = np.ascontiguousarray(np.asarray(L0_attn_q_norm_w, dtype=np.float32)[:, None])
    kn_bc = np.ascontiguousarray(np.asarray(L0_attn_k_norm_w, dtype=np.float32)[:, None])
    scales = np.ascontiguousarray(np.broadcast_to(
        np.array([float(np.asarray(L0_ffn_gate_s)[0]), float(np.asarray(L0_ffn_up_s)[0]),
                  float(np.asarray(L0_ffn_down_s)[0])], dtype=np.float32)[None, :], (128, 3)))
    return dict(gate_blk=gate_blk, up_blk=up_blk, down_blk=down_blk, qkv_blk=qkv_blk,
                wo_blk=wo_blk, qn_bc=qn_bc, kn_bc=kn_bc, scales=scales)


def _templates(c):
    jj = np.arange(512)[None, :]
    ii = np.arange(128)[:, None]
    out = np.empty((NJ, 128, 512), dtype=np.float32)
    for dd in range(NJ):
        rel = jj - ii - 512 * dd - 128 * c
        out[dd] = np.where(rel <= 0, rel, MASKVAL).astype(np.float32)
    return out


def _token_index(c):
    j = np.arange(NJ)[:, None]
    i = np.arange(128)[None, :]
    return (c * 128 + j * 512 + i).reshape(-1)


def kernel(L0_attn_x, L0_attn_Wq, L0_attn_Wk, L0_attn_Wv, L0_attn_Wo,
           L0_attn_q_norm_w, L0_attn_k_norm_w,
           L0_ffn_gate_w, L0_ffn_gate_s, L0_ffn_up_w, L0_ffn_up_s,
           L0_ffn_down_w, L0_ffn_down_s):
    x = np.asarray(L0_attn_x, dtype=np.float32)
    shared = _stage_shared(L0_attn_Wq, L0_attn_Wk, L0_attn_Wv, L0_attn_Wo,
                           L0_attn_q_norm_w, L0_attn_k_norm_w,
                           L0_ffn_gate_w, L0_ffn_gate_s, L0_ffn_up_w, L0_ffn_up_s,
                           L0_ffn_down_w, L0_ffn_down_s)
    nc = build_nc()
    in_maps = []
    for core in range(NCORE):
        s, c = divmod(core, 4)
        idx = _token_index(c)
        x_fm = np.ascontiguousarray(x[s][idx].T)
        m = dict(shared)
        m["x_fm"] = x_fm
        m["tmpl"] = _templates(c)
        in_maps.append(m)
    res = run_bass_kernel_spmd(nc, in_maps, core_ids=list(range(NCORE)))
    out = np.empty((B, S, DM), dtype=np.float32)
    for core in range(NCORE):
        s, c = divmod(core, 4)
        idx = _token_index(c)
        out[s][idx] = res.results[core]["out"].T
    return out


# version 09
# speedup vs baseline: 1.0475x; 1.0475x over previous
import contextlib
import math
import numpy as np
import ml_dtypes
import concourse.bass as bass
import concourse.mybir as mybir
from concourse.bass_utils import run_bass_kernel_spmd

F32 = mybir.dt.float32
BF16 = mybir.dt.bfloat16
I8 = mybir.dt.int8
AF = mybir.ActivationFunctionType
ALU = mybir.AluOpType
AX = mybir.AxisListType

B, S, DM = 2, 4096, 4096
H, HKV, HD = 32, 8, 128
FF = 11008
NCORE = 8
TPC = 1024
NJ = 8
NST = 2
STW = 512
DC = DM // 128
IC = FF // 128
GQ = H // HKV
LN_EPS = 1e-5
RMS_EPS = 1e-6
MASKVAL = -1e33

_AS = 2.0 ** (-(2.0 ** -(math.log2(H) - 3)))
SLOPES = [_AS * _AS ** i for i in range(H)]

ENGINES = ("sync", "gpsimd", "tensor", "vector", "scalar")
ATT_DEBUG = False
ALL_PHASES = ("ffn1", "qkv", "gather", "attn", "oproj", "ffn2")


class Prog:
    def __init__(self, ctx):
        self.ctx = ctx
        self.ops = {e: [] for e in ENGINES}
        self.trace = {e: [] for e in ENGINES}

    def wait(self, eng, sem, val):
        if val is None or val <= 0:
            return
        h = self.ctx.sem[sem]
        self.trace[eng].append(("w", sem, val))
        self.ops[eng].append(lambda e, h=h, v=val: e.wait_ge(h, v))

    def emit(self):
        self.ctx.traces.append(self.trace)
        with self.ctx.nc.Block() as b:
            for eng in ENGINES:
                lst = self.ops[eng]
                if not lst:
                    continue

                def body(e, lst=lst):
                    for f in lst:
                        f(e)
                getattr(b, eng)(body)


class Ring:

    def __init__(self, ctx, stack, name, nslots):
        self.ctx = ctx
        self.name = name
        self.slots = []
        for i in range(nslots):
            h = stack.enter_context(ctx.nc.semaphore(f"{name}{i}"))
            self.slots.append([h, 0])
        self.i = 0
        ctx.rings.append(self)

    def issue(self, p, eng, make, by=16, advance=True):
        sl = self.i % len(self.slots)
        if advance:
            self.i += 1
        self.slots[sl][1] += by
        h, c = self.slots[sl][0], self.slots[sl][1]
        p.trace[eng].append(("i", f"{self.name}{sl}", c))
        p.ops[eng].append(lambda e, mk=make, h=h, by=by: mk(e).then_inc(h, by))
        return (sl, c)

    def wait(self, p, eng, token):
        sl, c = token
        h = self.slots[sl][0]
        p.trace[eng].append(("w", f"{self.name}{sl}", c))
        p.ops[eng].append(lambda e, h=h, v=c: e.wait_ge(h, v))

    def waiter(self, token):
        return lambda p, eng, r=self, t=token: r.wait(p, eng, t)


def sem_waiter(sem, val):
    return lambda p, eng, sem=sem, val=val: p.wait(eng, sem, val)


class Ctx:
    def __init__(self, nc):
        self.nc = nc
        self.sem = {}
        self.cnt = {}
        self.t = {}
        self.dram = {}
        self.traces = []
        self.rings = []
        self.r = {}

    def check_deadlock(self):
        streams = {e: [] for e in ENGINES}
        for tr in self.traces:
            for e in ENGINES:
                streams[e].extend(tr[e])
        posn = {e: 0 for e in ENGINES}
        cnt = {}
        progress = True
        while progress:
            progress = False
            for e in ENGINES:
                while posn[e] < len(streams[e]):
                    kind, sem, val = streams[e][posn[e]]
                    if kind == "i":
                        cnt[sem] = max(cnt.get(sem, 0), val)
                        posn[e] += 1
                        progress = True
                    else:
                        if cnt.get(sem, 0) >= val:
                            posn[e] += 1
                            progress = True
                        else:
                            break
        stuck = {e: posn[e] for e in ENGINES if posn[e] < len(streams[e])}
        if stuck:
            detail = {e: streams[e][posn[e]] for e in stuck}
            raise RuntimeError(f"DEADLOCK: engines stuck at waits {detail}; "
                               f"counters {cnt}")
        return True

    def val(self, sem):
        return self.cnt[sem]


def op(p, ctx, eng, make, inc=None, by=1):
    if inc is None:
        p.ops[eng].append(lambda e, mk=make: mk(e))
        return None
    ctx.cnt[inc] += by
    h = ctx.sem[inc]
    p.trace[eng].append(("i", inc, ctx.cnt[inc]))
    p.ops[eng].append(lambda e, mk=make, h=h, by=by: mk(e).then_inc(h, by))
    return ctx.cnt[inc]


def barrier(p, ctx, engines=ENGINES):
    for eng in engines:
        for s in ("ld", "wq", "st", "pe", "dv", "ac", "gp", "cc"):
            if ctx.cnt[s] > 0:
                p.wait(eng, s, ctx.cnt[s])
        for r in ctx.rings:
            for sl, (h, c) in enumerate(r.slots):
                if c > 0:
                    p.trace[eng].append(("w", f"{r.name}{sl}", c))
                    p.ops[eng].append(lambda e, h=h, v=c: e.wait_ge(h, v))


def build_nc(phases=ALL_PHASES, debug=(), attn_io=False):
    nc = bass.Bass("TRN2", target_bir_lowering=False, debug=False)
    ctx = Ctx(nc)
    d = ctx.dram
    qkv_kind = "ExternalInput" if attn_io else "Internal"

    d["x_fm"] = nc.dram_tensor("x_fm", [DM, TPC], F32, kind="ExternalInput").ap()
    d["gate_blk"] = nc.dram_tensor("gate_blk", [IC, 128, DC * 128], I8, kind="ExternalInput").ap()
    d["up_blk"] = nc.dram_tensor("up_blk", [IC, 128, DC * 128], I8, kind="ExternalInput").ap()
    d["down_blk"] = nc.dram_tensor("down_blk", [DC, 128, IC * 128], I8, kind="ExternalInput").ap()
    d["qkv_blk"] = nc.dram_tensor("qkv_blk", [48, 128, DC * 128], F32, kind="ExternalInput").ap()
    d["wo_blk"] = nc.dram_tensor("wo_blk", [DC, 128, DC * 128], F32, kind="ExternalInput").ap()
    d["qn_bc"] = nc.dram_tensor("qn_bc", [128, 1], F32, kind="ExternalInput").ap()
    d["kn_bc"] = nc.dram_tensor("kn_bc", [128, 1], F32, kind="ExternalInput").ap()
    d["scales"] = nc.dram_tensor("scales", [128, 3], F32, kind="ExternalInput").ap()
    d["tmpl"] = nc.dram_tensor("tmpl", [NJ, 128, 512], F32, kind="ExternalInput").ap()

    d["x1_fm"] = nc.dram_tensor("x1_fm", [DM, TPC], F32, kind="Internal").ap()
    d["x2_fm"] = nc.dram_tensor("x2_fm", [DM, TPC], F32, kind="Internal").ap()
    d["xn2_fm"] = nc.dram_tensor("xn2_fm", [DM, TPC], BF16, kind="Internal").ap()
    d["xn3_fm"] = nc.dram_tensor("xn3_fm", [DM, TPC], BF16, kind="Internal").ap()
    d["qf"] = nc.dram_tensor("qf", [NJ, H * 128, 128], BF16, kind=qkv_kind).ap()
    d["k_in"] = nc.dram_tensor("k_in", [HKV * 128, TPC], BF16, kind=qkv_kind).ap()
    d["v_in"] = nc.dram_tensor("v_in", [HKV * TPC, 128], BF16, kind=qkv_kind).ap()
    d["k_g"] = nc.dram_tensor("k_g", [4 * HKV * 128, TPC], BF16, kind="Internal").ap()
    d["v_g"] = nc.dram_tensor("v_g", [4 * HKV * TPC, 128], BF16, kind="Internal").ap()
    d["attn"] = nc.dram_tensor("attn", [H * 128, TPC], BF16,
                               kind="ExternalOutput" if attn_io else "Internal").ap()
    d["out"] = nc.dram_tensor("out", [DM, TPC], F32, kind="ExternalOutput").ap()

    if ATT_DEBUG:
        for nm_, sh_, dt2 in (("att_ssp", [128, 1024], F32), ("att_psn", [128, 1024], F32),
                              ("att_ptp", [128, 1024], F32), ("att_rows", [128, 16], F32)):
            d[nm_] = nc.dram_tensor(nm_, sh_, dt2, kind="ExternalOutput").ap()
    dbg_specs = {
        "x1": ([DM, TPC], F32, "x1_fm"), "xn2": ([DM, TPC], F32, "xn2_fm"),
        "qf": ([NJ, H * 128, 128], F32, "qf"), "k_in": ([HKV * 128, TPC], F32, "k_in"),
        "v_in": ([HKV * TPC, 128], F32, "v_in"), "k_g": ([4 * HKV * 128, TPC], F32, "k_g"),
        "attn": ([H * 128, TPC], F32, "attn"), "x2": ([DM, TPC], F32, "x2_fm"),
        "xn3": ([DM, TPC], F32, "xn3_fm"),
    }
    for name in debug:
        sh, dt_, _ = dbg_specs[name]
        d["dbg_" + name] = nc.dram_tensor("dbg_" + name, sh, dt_, kind="ExternalOutput").ap()

    with contextlib.ExitStack() as _stk:
        for s in ("ld", "wq", "st", "pe", "dv", "ac", "gp", "cc"):
            ctx.sem[s] = _stk.enter_context(nc.semaphore("s_" + s))
            ctx.cnt[s] = 0
        sbufs = dict(
            ones_col=([128, 1], F32), ones_row=([1, 128], F32),
            qn_sb=([128, 1], F32), kn_sb=([128, 1], F32), scl_sb=([128, 3], F32),
            hds_sb=([128, 1], F32), ident=([128, 128], BF16), ones_bf=([128, 128], BF16),
            eps_ln=([1, 1], F32), eps_rms=([128, 1], F32),
        )
        for nm_, (sh_, dt_) in sbufs.items():
            ctx.t[nm_] = _stk.enter_context(nc.sbuf_tensor(nm_, sh_, dt_))
        ctx.r["xc"] = Ring(ctx, _stk, "rxc", 2)
        ctx.r["xr"] = Ring(ctx, _stk, "rxr", 2)
        ctx.r["wg"] = Ring(ctx, _stk, "rwg", 2)
        ctx.r["wd"] = Ring(ctx, _stk, "rwd", 2)
        ctx.r["so"] = Ring(ctx, _stk, "rso", 2)
        ctx.r["sx"] = Ring(ctx, _stk, "rsx", 2)
        ctx.r["wq2"] = Ring(ctx, _stk, "rwq", 2)
        ctx.r["sq"] = Ring(ctx, _stk, "rsq", 2)
        ctx.r["sv"] = Ring(ctx, _stk, "rsv", 2)
        ctx.r["kv"] = Ring(ctx, _stk, "rkv", 16)
        ctx.r["lq"] = Ring(ctx, _stk, "rlq", 2)
        ctx.r["sa"] = Ring(ctx, _stk, "rsa", 2)
        ctx.r["la"] = Ring(ctx, _stk, "rla", 2)
        ctx.r["wo3"] = Ring(ctx, _stk, "rwo", 3)

        emit_setup(ctx)
        if "ffn1" in phases:
            for st in range(NST):
                emit_ffn_supertile(ctx, st, xin=d["x_fm"], xres=d["x_fm"], xout=d["x1_fm"],
                                   ln_in=True, xn_next=d["xn2_fm"], tag=f"a{st}")
        if "qkv" in phases:
            emit_qkv(ctx)
        if "gather" in phases:
            emit_gather(ctx)
        if "attn" in phases:
            emit_attention(ctx)
        if "oproj" in phases:
            for st in range(NST):
                emit_oproj_supertile(ctx, st)
        if "ffn2" in phases:
            for st in range(NST):
                emit_ffn_supertile(ctx, st, xin=d["xn3_fm"], xres=d["x2_fm"], xout=d["out"],
                                   ln_in=False, xn_next=None, tag=f"b{st}")
        emit_tail(ctx, debug)
    ctx.check_deadlock()
    return nc


def emit_setup(ctx):
    t, d = ctx.t, ctx.dram
    p = Prog(ctx)
    ld1 = op(p, ctx, "sync", lambda e: e.dma_start(out=t["qn_sb"][:], in_=d["qn_bc"][:]), "ld", 16)
    ld2 = op(p, ctx, "sync", lambda e: e.dma_start(out=t["kn_sb"][:], in_=d["kn_bc"][:]), "ld", 16)
    ld3 = op(p, ctx, "sync", lambda e: e.dma_start(out=t["scl_sb"][:], in_=d["scales"][:]), "ld", 16)
    op(p, ctx, "gpsimd", lambda e: e.memset(t["eps_ln"][:], LN_EPS))
    op(p, ctx, "gpsimd", lambda e: e.memset(t["eps_rms"][:], RMS_EPS))
    op(p, ctx, "gpsimd", lambda e: e.memset(t["ones_col"][:], 1.0))
    op(p, ctx, "gpsimd", lambda e: e.memset(t["ones_row"][:], 1.0))
    gp_ones = op(p, ctx, "gpsimd", lambda e: e.memset(t["ones_bf"][:], 1.0), "gp")
    p.wait("gpsimd", "gp", gp_ones)
    op(p, ctx, "gpsimd", lambda e: e.affine_select(t["ident"][:], t["ones_bf"][:], pattern=[[1, 128]],
                                                   compare_op=ALU.is_equal, fill=0.0, base=0,
                                                   channel_multiplier=-1), "gp")
    p.wait("vector", "ld", ld3)
    op(p, ctx, "vector", lambda e: e.tensor_scalar_mul(t["hds_sb"][:], t["scl_sb"][:, 2:3], 0.5), "dv")
    op(p, ctx, "vector", lambda e: e.tensor_scalar_mul(t["qn_sb"][:], t["qn_sb"][:],
                                                       1.0 / math.sqrt(HD)), "dv")
    p.emit()


def emit_ln_finalize(p, ctx, rows, ps_s1, ps_s2, ps_bc, m_bc, r_bc, pe_stats_done):
    t = ctx.t
    p.wait("vector", "pe", pe_stats_done)
    op(p, ctx, "vector", lambda e: e.tensor_scalar_mul(rows[0:1, 0:512], ps_s1[0:1, :], 1.0 / DM), "dv")
    dv_t = op(p, ctx, "vector", lambda e: e.tensor_scalar_mul(rows[0:1, 512:1024], ps_s2[0:1, :], 1.0 / DM), "dv")
    p.wait("vector", "dv", dv_t)
    dv_t = op(p, ctx, "vector", lambda e: e.tensor_mul(rows[0:1, 1024:1536], rows[0:1, 0:512],
                                                       rows[0:1, 0:512]), "dv")
    p.wait("vector", "dv", dv_t)
    dv_var = op(p, ctx, "vector", lambda e: e.tensor_sub(rows[0:1, 1536:2048], rows[0:1, 512:1024],
                                                         rows[0:1, 1024:1536]), "dv")
    p.wait("scalar", "dv", dv_var)
    ac_sd = op(p, ctx, "scalar", lambda e: e.activation(rows[0:1, 1024:1536], rows[0:1, 1536:2048],
                                                        AF.Sqrt, bias=t["eps_ln"][0:1, 0:1]), "ac")
    p.wait("vector", "ac", ac_sd)
    dv_rstd = op(p, ctx, "vector", lambda e: e.reciprocal(rows[0:1, 512:1024],
                                                          rows[0:1, 1024:1536]), "dv")
    p.wait("tensor", "dv", dv_rstd)
    op(p, ctx, "tensor", lambda e: e.matmul(ps_bc[:, 0:512], t["ones_row"][0:1, :],
                                            rows[0:1, 0:512], start=True, stop=True), "pe")
    pe_bc = op(p, ctx, "tensor", lambda e: e.matmul(ps_bc[:, 512:1024], t["ones_row"][0:1, :],
                                                    rows[0:1, 512:1024], start=True, stop=True), "pe")
    p.wait("vector", "pe", pe_bc)
    op(p, ctx, "vector", lambda e: e.tensor_copy(m_bc[:], ps_bc[:, 0:512]), "dv")
    dv_bc = op(p, ctx, "vector", lambda e: e.tensor_copy(r_bc[:], ps_bc[:, 512:1024]), "dv")
    return dv_bc


def emit_ln_normalize(p, ctx, xsrc, c0, xn_write, xc, tmp, m_bc, r_bc, dv_bc,
                      src_waiters=()):
    rxc, rsx = ctx.r["xc"], ctx.r["sx"]
    dv_n = {}
    st_n = {}
    for dc in range(DC):
        sl = (dc % 2) * 512
        if dc == 0:
            for w in src_waiters:
                w(p, "sync")
        if dc >= 2:
            p.wait("sync", "dv", dv_n[dc - 2])
        tk_x = rxc.issue(p, "sync", lambda e, dc=dc, sl=sl: e.dma_start(
            out=xc[:, sl:sl + 512], in_=xsrc[dc * 128:(dc + 1) * 128, c0:c0 + STW]))
        dest, sb_out = xn_write(dc)
        rxc.wait(p, "vector", tk_x)
        if dc == 0:
            p.wait("vector", "dv", dv_bc)
        if dest is not None and dc >= 2:
            rsx.wait(p, "vector", st_n[dc - 2])
        dv_t = op(p, ctx, "vector", lambda e, sl=sl: e.tensor_sub(tmp[:, sl:sl + 512],
                                                                  xc[:, sl:sl + 512], m_bc[:]), "dv")
        p.wait("vector", "dv", dv_t)
        dv_n[dc] = op(p, ctx, "vector", lambda e, sb_out=sb_out, sl=sl: e.tensor_mul(
            sb_out, tmp[:, sl:sl + 512], r_bc[:]), "dv")
        if dest is not None:
            p.wait("sync", "dv", dv_n[dc])
            st_n[dc] = rsx.issue(p, "sync", lambda e, dest=dest, sb_out=sb_out: e.dma_start(
                out=dest, in_=sb_out))


def emit_ffn_supertile(ctx, st, xin, xres, xout, ln_in, xn_next, tag):
    nc, t, d = ctx.nc, ctx.t, ctx.dram
    c0 = st * STW

    with (
        nc.sbuf_tensor(f"xn_{tag}", [128, DC * STW], BF16) as xn_fm,
        nc.sbuf_tensor(f"h_{tag}", [128, IC * STW], BF16) as h_fm,
    ):
        if ln_in:
            with (
                nc.sbuf_tensor(f"xa_{tag}", [128, 2 * 512], F32) as xc,
                nc.sbuf_tensor(f"sq_{tag}", [128, 2 * 512], F32) as sq,
                nc.sbuf_tensor(f"mb_{tag}", [128, 512], F32) as m_bc,
                nc.sbuf_tensor(f"rb_{tag}", [128, 512], F32) as r_bc,
                nc.sbuf_tensor(f"rw_{tag}", [1, 2048], F32) as rows,
                nc.psum_tensor(f"p1_{tag}", [128, 512], F32) as ps_s1,
                nc.psum_tensor(f"p2_{tag}", [128, 512], F32) as ps_s2,
                nc.psum_tensor(f"pb_{tag}", [128, 1024], F32) as ps_bc,
            ):
                p = Prog(ctx)
                barrier(p, ctx)
                rxc = ctx.r["xc"]
                ld_x = {}
                ac_sq = {}
                pe_mm = {}
                for dc in range(DC):
                    sl = (dc % 2) * 512
                    if dc >= 2:
                        p.wait("sync", "pe", pe_mm[dc - 2])
                    ld_x[dc] = rxc.issue(p, "sync", lambda e, dc=dc, sl=sl: e.dma_start(
                        out=xc[:, sl:sl + 512], in_=xin[dc * 128:(dc + 1) * 128, c0:c0 + STW]))
                    rxc.wait(p, "scalar", ld_x[dc])
                    if dc >= 2:
                        p.wait("scalar", "pe", pe_mm[dc - 2])
                    ac_sq[dc] = op(p, ctx, "scalar", lambda e, sl=sl: e.activation(
                        sq[:, sl:sl + 512], xc[:, sl:sl + 512], AF.Square), "ac")
                    p.wait("tensor", "ac", ac_sq[dc])
                    op(p, ctx, "tensor", lambda e, sl=sl, dc=dc: e.matmul(
                        ps_s1[0:1, :], t["ones_col"][:], xc[:, sl:sl + 512],
                        start=(dc == 0), stop=(dc == DC - 1)))
                    pe_mm[dc] = op(p, ctx, "tensor", lambda e, sl=sl, dc=dc: e.matmul(
                        ps_s2[0:1, :], t["ones_col"][:], sq[:, sl:sl + 512],
                        start=(dc == 0), stop=(dc == DC - 1)), "pe")
                dv_bc = emit_ln_finalize(p, ctx, rows, ps_s1, ps_s2, ps_bc, m_bc, r_bc,
                                         pe_stats_done=pe_mm[DC - 1])

                def xn_write(dc):
                    return None, xn_fm[:, dc * STW:(dc + 1) * STW]
                emit_ln_normalize(p, ctx, xin, c0, xn_write, xc, sq, m_bc, r_bc, dv_bc,
                                  src_waiters=(sem_waiter("pe", pe_mm[DC - 1]),))
                p.emit()
            xn_ready = ("dv", ctx.val("dv"))
        else:
            p = Prog(ctx)
            barrier(p, ctx)
            last = None
            for dc in range(DC):
                last = op(p, ctx, "sync", lambda e, dc=dc: e.dma_start(
                    out=xn_fm[:, dc * STW:(dc + 1) * STW],
                    in_=xin[dc * 128:(dc + 1) * 128, c0:c0 + STW]), "ld", 16)
            p.emit()
            xn_ready = ("ld", last)

        with (
            nc.sbuf_tensor(f"gw_{tag}", [128, 2 * DC * 128], BF16) as gw,
            nc.sbuf_tensor(f"uw_{tag}", [128, 2 * DC * 128], BF16) as uw,
            nc.sbuf_tensor(f"si_{tag}", [128, 512], F32) as silu_sb,
            nc.sbuf_tensor(f"us_{tag}", [128, 512], F32) as up_sb,
            nc.psum_tensor(f"pg_{tag}", [128, 1024], F32) as ps_g,
            nc.psum_tensor(f"pu_{tag}", [128, 1024], F32) as ps_u,
        ):
            p = Prog(ctx)
            wq_w = {}
            pe_u = {}
            ac_su = {}
            dv_h = {}
            p.wait("tensor", xn_ready[0], xn_ready[1])
            p.wait("gpsimd", xn_ready[0], xn_ready[1])
            for ic in range(IC):
                sl = (ic % 2) * DC * 128
                pb = (ic % 2) * 512
                if ic >= 2:
                    p.wait("gpsimd", "pe", pe_u[ic - 2])
                rwg = ctx.r["wg"]
                rwg.issue(p, "gpsimd", lambda e, ic=ic, sl=sl: e.dma_start(
                    out=gw[:, sl:sl + DC * 128], in_=d["gate_blk"][ic]), advance=False)
                wq_w[ic] = rwg.issue(p, "gpsimd", lambda e, ic=ic, sl=sl: e.dma_start(
                    out=uw[:, sl:sl + DC * 128], in_=d["up_blk"][ic]))

                rwg.wait(p, "tensor", wq_w[ic])
                if ic >= 2:
                    p.wait("tensor", "ac", ac_su[ic - 2])
                for dc in range(DC):
                    last = dc == DC - 1
                    op(p, ctx, "tensor", lambda e, sl=sl, pb=pb, dc=dc, last=last: e.matmul(
                        ps_g[:, pb:pb + 512], gw[:, sl + dc * 128: sl + dc * 128 + 128],
                        xn_fm[:, dc * STW:(dc + 1) * STW], start=(dc == 0), stop=last),
                        "pe" if last else None)
                    pe_u[ic] = op(p, ctx, "tensor", lambda e, sl=sl, pb=pb, dc=dc, last=last: e.matmul(
                        ps_u[:, pb:pb + 512], uw[:, sl + dc * 128: sl + dc * 128 + 128],
                        xn_fm[:, dc * STW:(dc + 1) * STW], start=(dc == 0), stop=last),
                        "pe" if last else None)

                p.wait("scalar", "pe", pe_u[ic])
                if ic >= 1:
                    p.wait("scalar", "dv", dv_h[ic - 1])
                op(p, ctx, "scalar", lambda e, pb=pb: e.activation(
                    silu_sb[:], ps_g[:, pb:pb + 512], AF.Silu, scale=t["scl_sb"][:, 0:1]), "ac")
                ac_su[ic] = op(p, ctx, "scalar", lambda e, pb=pb: e.activation(
                    up_sb[:], ps_u[:, pb:pb + 512], AF.Copy, scale=t["scl_sb"][:, 1:2]), "ac")

                p.wait("vector", "ac", ac_su[ic])
                dv_h[ic] = op(p, ctx, "vector", lambda e, ic=ic: e.tensor_mul(
                    h_fm[:, ic * STW:(ic + 1) * STW], silu_sb[:], up_sb[:]), "dv")
            p.emit()
        h_done = ctx.val("dv")

        do_ln = xn_next is not None
        with (
            nc.sbuf_tensor(f"dw_{tag}", [128, 2 * 43 * 128], BF16) as dw,
            nc.sbuf_tensor(f"xr_{tag}", [128, 2 * 512], F32) as xr,
            nc.sbuf_tensor(f"oc_{tag}", [128, 2 * 512], F32) as oc,
            nc.sbuf_tensor(f"sq2_{tag}", [128, 2 * 512], F32) as sqc,
            nc.sbuf_tensor(f"mb2_{tag}", [128, 512], F32) as m_bc2,
            nc.sbuf_tensor(f"rb2_{tag}", [128, 512], F32) as r_bc2,
            nc.sbuf_tensor(f"rw2_{tag}", [1, 2048], F32) as rows2,
            nc.sbuf_tensor(f"xnc_{tag}", [128, 2 * 512], F32) as xnc,
            nc.sbuf_tensor(f"xno_{tag}", [128, 2 * 512], BF16) as xno,
            nc.psum_tensor(f"pd_{tag}", [128, 1024], F32) as ps_d,
            nc.psum_tensor(f"p1b_{tag}", [128, 512], F32) as ps_s1,
            nc.psum_tensor(f"p2b_{tag}", [128, 512], F32) as ps_s2,
            nc.psum_tensor(f"pbb_{tag}", [128, 1024], F32) as ps_bc,
        ):
            p = Prog(ctx)
            p.wait("tensor", "dv", h_done)
            p.wait("gpsimd", "dv", h_done)
            p.wait("sync", "dv", h_done)
            wq_h = {}
            pe_half = {}
            pe_stats = {}
            dv_oc = {}
            ac_sq = {}
            st_oc = {}
            ld_xr = {}

            def emit_stats(sdc, start, stop):
                spb = (sdc % 2) * 512
                p.wait("tensor", "ac", ac_sq[sdc])
                op(p, ctx, "tensor", lambda e, spb=spb: e.matmul(
                    ps_s1[0:1, :], t["ones_col"][:], oc[:, spb:spb + 512],
                    start=start, stop=stop))
                pe_stats[sdc] = op(p, ctx, "tensor", lambda e, spb=spb: e.matmul(
                    ps_s2[0:1, :], t["ones_col"][:], sqc[:, spb:spb + 512],
                    start=start, stop=stop), "pe")

            for dc in range(DC):
                pb = (dc % 2) * 512
                for hf in range(2):
                    k = dc * 2 + hf
                    sl = (k % 2) * 43 * 128
                    if k >= 2:
                        p.wait("gpsimd", "pe", pe_half[k - 2])
                    wq_h[k] = ctx.r["wd"].issue(p, "gpsimd", lambda e, dc=dc, hf=hf, sl=sl: e.dma_start(
                        out=dw[:, sl:sl + 43 * 128],
                        in_=d["down_blk"][dc, :, hf * 43 * 128:(hf + 1) * 43 * 128]))

                if dc >= 2:
                    p.wait("sync", "dv", dv_oc[dc - 2])
                ld_xr[dc] = ctx.r["xr"].issue(p, "sync", lambda e, dc=dc, pb=pb: e.dma_start(
                    out=xr[:, pb:pb + 512], in_=xres[dc * 128:(dc + 1) * 128, c0:c0 + STW]))

                if do_ln and dc >= 2:
                    emit_stats(dc - 2, start=(dc - 2 == 0), stop=False)

                if dc >= 2:
                    p.wait("tensor", "dv", dv_oc[dc - 2])
                for hf in range(2):
                    k = dc * 2 + hf
                    sl = (k % 2) * 43 * 128
                    ctx.r["wd"].wait(p, "tensor", wq_h[k])
                    for i in range(43):
                        ic = hf * 43 + i
                        pe_half[k] = op(p, ctx, "tensor", lambda e, sl=sl, pb=pb, i=i, ic=ic: e.matmul(
                            ps_d[:, pb:pb + 512], dw[:, sl + i * 128: sl + i * 128 + 128],
                            h_fm[:, ic * STW:(ic + 1) * STW], start=(ic == 0), stop=(ic == IC - 1)),
                            "pe" if i == 42 else None)

                p.wait("vector", "pe", pe_half[dc * 2 + 1])
                ctx.r["xr"].wait(p, "vector", ld_xr[dc])
                if dc >= 2:
                    ctx.r["so"].wait(p, "vector", st_oc[dc - 2])
                    if do_ln:
                        p.wait("vector", "pe", pe_stats[dc - 2])
                dv_oc[dc] = op(p, ctx, "vector", lambda e, pb=pb: e.scalar_tensor_tensor(
                    oc[:, pb:pb + 512], ps_d[:, pb:pb + 512], t["hds_sb"][:, 0:1],
                    xr[:, pb:pb + 512], ALU.mult, ALU.add), "dv")

                if do_ln:
                    p.wait("scalar", "dv", dv_oc[dc])
                    if dc >= 2:
                        p.wait("scalar", "pe", pe_stats[dc - 2])
                    ac_sq[dc] = op(p, ctx, "scalar", lambda e, pb=pb: e.activation(
                        sqc[:, pb:pb + 512], oc[:, pb:pb + 512], AF.Square), "ac")

                p.wait("sync", "dv", dv_oc[dc])
                st_oc[dc] = ctx.r["so"].issue(p, "sync", lambda e, dc=dc, pb=pb: e.dma_start(
                    out=xout[dc * 128:(dc + 1) * 128, c0:c0 + STW], in_=oc[:, pb:pb + 512]))

            if do_ln:
                emit_stats(DC - 2, start=False, stop=False)
                emit_stats(DC - 1, start=False, stop=True)
                dv_bc = emit_ln_finalize(p, ctx, rows2, ps_s1, ps_s2, ps_bc, m_bc2, r_bc2,
                                         pe_stats_done=pe_stats[DC - 1])

                def xn_write(dc):
                    sl = (dc % 2) * 512
                    return (xn_next[dc * 128:(dc + 1) * 128, c0:c0 + STW], xno[:, sl:sl + 512])
                emit_ln_normalize(p, ctx, xout, c0, xn_write, xnc, sqc, m_bc2, r_bc2, dv_bc,
                                  src_waiters=(ctx.r["so"].waiter(st_oc[DC - 2]),
                                               ctx.r["so"].waiter(st_oc[DC - 1]),
                                               sem_waiter("pe", pe_stats[DC - 1])))
            p.emit()


def emit_qkv(ctx):
    nc, t, d = ctx.nc, ctx.t, ctx.dram
    with (
        nc.sbuf_tensor("xn2_sb", [128, DC * TPC], BF16) as xn2_sb,
        nc.sbuf_tensor("qkvw", [128, 2 * DC * 128], BF16) as wsb,
        nc.sbuf_tensor("sqq", [128, 2 * 512], F32) as sqq,
        nc.sbuf_tensor("rmsr", [1, 4 * 512], F32) as rmsr,
        nc.sbuf_tensor("rbcq", [128, 2 * 512], F32) as rbc,
        nc.sbuf_tensor("qbf", [128, 2 * 512], BF16) as qbf,
        nc.sbuf_tensor("vtr", [128, 2 * 512], BF16) as vtr,
        nc.psum_tensor("ps_qkv", [128, 4 * 512], F32) as ps_q,
        nc.psum_tensor("ps_rs", [128, 512], F32) as ps_rs,
        nc.psum_tensor("ps_rb", [128, 512], F32) as ps_rb,
        nc.psum_tensor("ps_vt", [128, 2048], BF16) as ps_vt,
    ):
        p = Prog(ctx)
        barrier(p, ctx)
        ld_xn = None
        for dc in range(DC):
            ld_xn = op(p, ctx, "sync", lambda e, dc=dc: e.dma_start(
                out=xn2_sb[:, dc * TPC:(dc + 1) * TPC],
                in_=d["xn2_fm"][dc * 128:(dc + 1) * 128, :]), "ld", 16)
        p.wait("tensor", "ld", ld_xn)

        wq_c = {}
        pe_mm = {}
        ac_sq = {}
        pe_s2 = {}
        dv_row = {}
        dv_rstd = {}
        pe_bc = {}
        srb_free = {}
        psum_free = {}
        qbf_free = {}
        dv_cp_v = {}
        pe_tr = {}
        pvt_free = {}
        vtr_free = {}
        NQS = 4

        def stage1(k):
            cc, th = divmod(k, 2)
            pb = (k % NQS) * 512
            sl2 = (k % 2) * 512
            if cc < 40:
                p.wait("scalar", "pe", pe_mm[k])
                if k >= 2 and (k - 2) in pe_s2:
                    p.wait("scalar", "pe", pe_s2[k - 2])
                ac_sq[k] = op(p, ctx, "scalar", lambda e, pb=pb, sl2=sl2: e.activation(
                    sqq[:, sl2:sl2 + 512], ps_q[:, pb:pb + 512], AF.Square), "ac")
            else:
                p.wait("vector", "pe", pe_mm[k])
                if k >= 2:
                    qbf_free[k - 2](p, "vector")
                dv_cp_v[k] = op(p, ctx, "vector", lambda e, pb=pb, sl2=sl2: e.tensor_copy(
                    qbf[:, sl2:sl2 + 512], ps_q[:, pb:pb + 512]), "dv")
                psum_free[k] = sem_waiter("dv", dv_cp_v[k])

        def stage2(k):
            cc, th = divmod(k, 2)
            sl2 = (k % 2) * 512
            if cc < 40:
                p.wait("tensor", "ac", ac_sq[k])
                if k >= 1 and (k - 1) in dv_row:
                    p.wait("tensor", "dv", dv_row[k - 1])
                pe_s2[k] = op(p, ctx, "tensor", lambda e, sl2=sl2: e.matmul(
                    ps_rs[0:1, :], t["ones_col"][:], sqq[:, sl2:sl2 + 512],
                    start=True, stop=True), "pe")
                p.wait("vector", "pe", pe_s2[k])
                if k >= 2 and (k - 2) in pe_bc:
                    p.wait("vector", "pe", pe_bc[k - 2])
                dv_row[k] = op(p, ctx, "vector", lambda e, sl2=sl2: e.tensor_scalar_mul(
                    rmsr[0:1, sl2:sl2 + 512], ps_rs[0:1, :], 1.0 / HD), "dv")
                p.wait("scalar", "dv", dv_row[k])
                ac_sd = op(p, ctx, "scalar", lambda e, sl2=sl2: e.activation(
                    rmsr[0:1, 1024 + sl2:1024 + sl2 + 512], rmsr[0:1, sl2:sl2 + 512], AF.Sqrt,
                    bias=t["eps_rms"][0:1, 0:1]), "ac")
                p.wait("vector", "ac", ac_sd)
                dv_rstd[k] = op(p, ctx, "vector", lambda e, sl2=sl2: e.reciprocal(
                    rmsr[0:1, sl2:sl2 + 512], rmsr[0:1, 1024 + sl2:1024 + sl2 + 512]), "dv")
            else:
                p.wait("tensor", "dv", dv_cp_v[k])
                if k >= 2 and (k - 2) in pvt_free:
                    pvt_free[k - 2](p, "tensor")
                pt = None
                for b_ in range(4):
                    pt = op(p, ctx, "tensor", lambda e, sl2=sl2, b_=b_, k=k: e.transpose(
                        ps_vt[:, (k % 2) * 1024 + b_ * 128:(k % 2) * 1024 + b_ * 128 + 128],
                        qbf[:, sl2 + b_ * 128: sl2 + b_ * 128 + 128], t["ident"][:]),
                        "pe" if b_ == 3 else None)
                pe_tr[k] = pt
                qbf_free[k] = sem_waiter("pe", pt)
                p.wait("vector", "pe", pt)
                if k >= 2 and (k - 2) in vtr_free:
                    vtr_free[k - 2](p, "vector")
                dv_ev = op(p, ctx, "vector", lambda e, sl2=sl2, k=k: e.tensor_copy(
                    vtr[:, sl2:sl2 + 512], ps_vt[:, (k % 2) * 1024:(k % 2) * 1024 + 512]), "dv")
                pvt_free[k] = sem_waiter("dv", dv_ev)
                p.wait("sync", "dv", dv_ev)
                kv = cc - 40
                dst = d["v_in"][kv * TPC + th * 512: kv * TPC + (th + 1) * 512, :].rearrange(
                    "(b p) hd -> p b hd", p=128)
                st_o = ctx.r["sv"].issue(p, "sync", lambda e, dst=dst, sl2=sl2: e.dma_start(
                    out=dst, in_=vtr[:, sl2:sl2 + 512]))
                vtr_free[k] = ctx.r["sv"].waiter(st_o)

        def stage3(k):
            cc, th = divmod(k, 2)
            if cc >= 40:
                return
            pb = (k % NQS) * 512
            sl2 = (k % 2) * 512
            p.wait("tensor", "dv", dv_rstd[k])
            if k >= 1 and (k - 1) in srb_free:
                srb_free[k - 1](p, "tensor")
            pe_bc[k] = op(p, ctx, "tensor", lambda e, sl2=sl2: e.matmul(
                ps_rb[:, :], t["ones_row"][0:1, :], rmsr[0:1, sl2:sl2 + 512],
                start=True, stop=True), "pe")
            p.wait("vector", "pe", pe_bc[k])
            dv_cp = op(p, ctx, "vector", lambda e, sl2=sl2: e.tensor_copy(
                rbc[:, sl2:sl2 + 512], ps_rb[:, :]), "dv")
            srb_free[k] = sem_waiter("dv", dv_cp)
            p.wait("vector", "dv", dv_cp)
            if k >= 2:
                qbf_free[k - 2](p, "vector")
            w_ap = t["qn_sb"] if cc < 32 else t["kn_sb"]
            dv_stt = op(p, ctx, "vector", lambda e, pb=pb, sl2=sl2, w_ap=w_ap: e.scalar_tensor_tensor(
                qbf[:, sl2:sl2 + 512], ps_q[:, pb:pb + 512], w_ap[:, 0:1],
                rbc[:, sl2:sl2 + 512], ALU.mult, ALU.mult), "dv")
            psum_free[k] = sem_waiter("dv", dv_stt)
            p.wait("sync", "dv", dv_stt)
            if cc < 32:
                dst = d["qf"][:, cc * 128:(cc + 1) * 128, :].rearrange(
                    "j hd q -> hd j q")[:, th * 4:(th + 1) * 4, :]
            else:
                kv = cc - 32
                dst = d["k_in"][kv * 128:(kv + 1) * 128, th * 512:(th + 1) * 512]
            st_o = ctx.r["sq"].issue(p, "sync", lambda e, dst=dst, sl2=sl2: e.dma_start(
                out=dst, in_=qbf[:, sl2:sl2 + 512]))
            qbf_free[k] = ctx.r["sq"].waiter(st_o)

        for cc in range(48):
            if cc >= 2:
                p.wait("gpsimd", "pe", pe_mm[(cc - 2) * 2 + 1])
            wq_c[cc] = ctx.r["wq2"].issue(p, "gpsimd", lambda e, cc=cc: e.dma_start(
                out=wsb[:, (cc % 2) * DC * 128:(cc % 2) * DC * 128 + DC * 128],
                in_=d["qkv_blk"][cc]))
            for th in range(2):
                k = cc * 2 + th
                pb = (k % NQS) * 512
                wsl = (cc % 2) * DC * 128
                ctx.r["wq2"].wait(p, "tensor", wq_c[cc])
                if k >= NQS:
                    psum_free[k - NQS](p, "tensor")
                for dc in range(DC):
                    pe_mm[k] = op(p, ctx, "tensor", lambda e, wsl=wsl, pb=pb, dc=dc, th=th: e.matmul(
                        ps_q[:, pb:pb + 512], wsb[:, wsl + dc * 128: wsl + dc * 128 + 128],
                        xn2_sb[:, dc * TPC + th * 512: dc * TPC + th * 512 + 512],
                        start=(dc == 0), stop=(dc == DC - 1)), "pe" if dc == DC - 1 else None)
                if k >= 1:
                    stage2(k - 1)
                if k >= 2:
                    stage3(k - 2)
                stage1(k)
        stage2(95)
        stage3(94)
        stage3(95)
        p.emit()


def emit_gather(ctx):
    nc, d = ctx.nc, ctx.dram
    p = Prog(ctx)
    barrier(p, ctx, engines=("gpsimd",))
    groups = [[0, 1, 2, 3], [4, 5, 6, 7]]
    cc1 = op(p, ctx, "gpsimd", lambda e: e.collective_compute(
        "AllGather", ALU.bypass, replica_groups=groups,
        ins=[d["k_in"]], outs=[d["k_g"]]), "cc")
    cc2 = op(p, ctx, "gpsimd", lambda e: e.collective_compute(
        "AllGather", ALU.bypass, replica_groups=groups,
        ins=[d["v_in"]], outs=[d["v_g"]]), "cc")
    p.wait("gpsimd", "cc", cc2)
    p.emit()


def emit_attention(ctx):
    nc, t, d = ctx.nc, ctx.t, ctx.dram
    with (
        nc.sbuf_tensor("tmpl_sb", [128, NJ * 512], F32) as tmpl_sb,
        nc.sbuf_tensor("qj_sb", [128, H * 128], BF16) as qj,
        nc.sbuf_tensor("ksp", [128, 2 * NJ * 512], BF16) as ksp,
        nc.sbuf_tensor("vsp", [128, 2 * NJ * 512], BF16) as vsp,
        nc.sbuf_tensor("ssp", [128, 2 * 4096], F32) as ssp,
        nc.sbuf_tensor("psp", [128, 2 * 4096], BF16) as psp,
        nc.sbuf_tensor("psn", [128, 2 * 4096], BF16) as psn,
        nc.sbuf_tensor("ptp", [128, NJ * 1024], BF16) as ptp,
        nc.sbuf_tensor("nmx", [128, 8], F32) as nmx,
        nc.sbuf_tensor("rsm", [128, 8], F32) as rsm,
        nc.sbuf_tensor("rcp", [128, 8], F32) as rcp,
        nc.sbuf_tensor("rse", [128, 2], F32) as rse,
        nc.sbuf_tensor("obf", [128, 2 * H * 128], BF16) as obf,
        nc.psum_tensor("ps_sc", [128, 2048], F32) as ps_sc,
        nc.psum_tensor("ps_pt", [128, 2048], BF16) as ps_pt,
        nc.psum_tensor("ps_o", [128, 1024], F32) as ps_o,
    ):
        p = Prog(ctx)
        barrier(p, ctx)
        ld_tm = op(p, ctx, "sync", lambda e: e.dma_start(
            out=tmpl_sb[:].rearrange("p (dd c) -> p dd c", dd=NJ),
            in_=d["tmpl"].rearrange("dd p c -> p dd c")), "ld", 16)
        p.wait("vector", "ld", ld_tm)

        G = dict(last_qk=None, last_pv=None, sc_guard={}, pt_guard={}, exp_guard={},
                 pn_guard={}, psn_guard={}, o_guard={}, obf_guard={}, nm_guard={},
                 rs_guard={}, rc_guard={}, pp=0, qk_par={}, pv_par={})

        def attn_pair(j, g, pr, ld_q, ld_k, ld_v, gb):
            span = (j + 1) * 512
            kb = gb * NJ * 512
            pp = G["pp"]
            h0 = g * 4 + 2 * pr
            dv_sc = {}
            pe_qk = {}
            for kc in range(j + 1):
                scb = (kc % 2) * 1024
                ctx.r["kv"].wait(p, "tensor", ld_k[kc])
                ctx.r["lq"].wait(p, "tensor", ld_q)
                if (kc % 2) in G["sc_guard"]:
                    p.wait("tensor", "dv", G["sc_guard"][kc % 2])
                for hh in range(2):
                    h = h0 + hh
                    pe_qk[kc] = op(p, ctx, "tensor", lambda e, scb=scb, h=h, hh=hh, kc=kc, kb=kb: e.matmul(
                        ps_sc[:, scb + hh * 512: scb + hh * 512 + 512],
                        qj[:, h * 128:(h + 1) * 128], ksp[:, kb + kc * 512: kb + (kc + 1) * 512],
                        start=True, stop=True), "pe" if hh == 1 else None)
                G["last_qk"] = G["qk_par"][gb] = pe_qk[kc]
                p.wait("vector", "pe", pe_qk[kc])
                if kc == 0:
                    for hh in range(2):
                        if hh in G["exp_guard"]:
                            p.wait("vector", "ac", G["exp_guard"][hh])
                for hh in range(2):
                    h = h0 + hh
                    dv_sc[kc] = op(p, ctx, "vector", lambda e, scb=scb, hh=hh, kc=kc, j=j, h=h: e.scalar_tensor_tensor(
                        ssp[:, hh * 4096 + kc * 512: hh * 4096 + kc * 512 + 512],
                        tmpl_sb[:, (j - kc) * 512:(j - kc + 1) * 512], float(SLOPES[h]),
                        ps_sc[:, scb + hh * 512: scb + hh * 512 + 512],
                        ALU.mult, ALU.add), "dv" if hh == 1 else None)
                G["sc_guard"][kc % 2] = dv_sc[kc]

            p.wait("vector", "dv", dv_sc[j])
            if pp % 2 in G["nm_guard"]:
                p.wait("vector", "ac", G["nm_guard"][pp % 2])
            dv_nm = {}
            for hh in range(2):
                col = (pp % 2) * 2 + hh
                dv_nm[hh] = op(p, ctx, "vector", lambda e, hh=hh, col=col, span=span: e.reduce_max(
                    nmx[:, col:col + 1], ssp[:, hh * 4096: hh * 4096 + span],
                    axis=AX.X, negate=True), "dv")
            for hh in range(2):
                if hh in G["pn_guard"]:
                    p.wait("scalar", "dv", G["pn_guard"][hh])
            if pp % 2 in G["rs_guard"]:
                p.wait("scalar", "dv", G["rs_guard"][pp % 2])
            ac_e2 = None
            for hh in range(2):
                col = (pp % 2) * 2 + hh
                p.wait("scalar", "dv", dv_nm[hh])
                ac_e2 = op(p, ctx, "scalar", lambda e, hh=hh, col=col, span=span: e.activation(
                    psp[:, hh * 4096: hh * 4096 + span], ssp[:, hh * 4096: hh * 4096 + span],
                    AF.Exp, bias=nmx[:, col:col + 1], accum_out=rsm[:, col:col + 1]), "ac")
            p.wait("scalar", "ac", ac_e2)
            ac_exp = op(p, ctx, "scalar", lambda e, pp=pp: e.activation(
                rse[:, 0:2], rsm[:, (pp % 2) * 2:(pp % 2) * 2 + 2], AF.Copy), "ac")
            G["exp_guard"][0] = G["exp_guard"][1] = ac_exp
            G["nm_guard"][pp % 2] = ac_exp
            p.wait("vector", "ac", ac_exp)
            if pp % 2 in G["rc_guard"]:
                p.wait("vector", "dv", G["rc_guard"][pp % 2])
            dv_rc = op(p, ctx, "vector", lambda e, pp=pp: e.reciprocal(
                rcp[:, (pp % 2) * 2:(pp % 2) * 2 + 2], rsm[:, (pp % 2) * 2:(pp % 2) * 2 + 2]), "dv")
            G["rs_guard"][pp % 2] = dv_rc
            p.wait("vector", "dv", dv_rc)
            dv_pn = {}
            for hh in range(2):
                col = (pp % 2) * 2 + hh
                if hh in G["psn_guard"]:
                    p.wait("vector", "pe", G["psn_guard"][hh])
                dv_pn[hh] = op(p, ctx, "vector", lambda e, hh=hh, col=col, span=span: e.tensor_scalar(
                    psn[:, hh * 4096: hh * 4096 + span], psp[:, hh * 4096: hh * 4096 + span],
                    rcp[:, col:col + 1], None, ALU.mult), "dv")
            G["rc_guard"][pp % 2] = dv_pn[1]

            dv_pt = {}
            pe_pt = {}
            for kc in range(j + 1):
                ptb = (kc % 2) * 1024
                if kc == 0:
                    p.wait("tensor", "dv", dv_pn[1])
                    if G["last_pv"] is not None:
                        p.wait("vector", "pe", G["last_pv"])
                if (kc % 2) in G["pt_guard"]:
                    p.wait("tensor", "ac", G["pt_guard"][kc % 2])
                pe_l = None
                for sub in range(4):
                    for hh in range(2):
                        pe_l = op(p, ctx, "tensor", lambda e, ptb=ptb, sub=sub, hh=hh, kc=kc: e.transpose(
                            ps_pt[:, ptb + (sub * 2 + hh) * 128: ptb + (sub * 2 + hh) * 128 + 128],
                            psn[:, hh * 4096 + kc * 512 + sub * 128: hh * 4096 + kc * 512 + sub * 128 + 128],
                            t["ident"][:]), "pe" if (sub == 3 and hh == 1) else None)
                pe_pt[kc] = pe_l
                p.wait("scalar", "pe", pe_pt[kc])
                dv_pt[kc] = op(p, ctx, "scalar", lambda e, ptb=ptb, kc=kc: e.copy(
                    ptp[:, kc * 1024:(kc + 1) * 1024], ps_pt[:, ptb:ptb + 1024]), "ac")
                G["pt_guard"][kc % 2] = dv_pt[kc]
            G["pn_guard"][0] = dv_pn[0]
            G["pn_guard"][1] = dv_pn[1]
            G["psn_guard"][0] = G["psn_guard"][1] = pe_pt[j]

            ob = (pp % 2) * 512
            if pp % 2 in G["o_guard"]:
                p.wait("tensor", "dv", G["o_guard"][pp % 2])
            pe_pv = None
            for kc in range(j + 1):
                p.wait("tensor", "ac", dv_pt[kc])
                ctx.r["kv"].wait(p, "tensor", ld_v[kc])
                for sub in range(4):
                    pe_pv = op(p, ctx, "tensor", lambda e, kc=kc, sub=sub, ob=ob, j=j, kb=kb: e.matmul(
                        ps_o[:, ob:ob + 256], vsp[:, kb + (kc * 4 + sub) * 128: kb + (kc * 4 + sub) * 128 + 128],
                        ptp[:, (kc * 4 + sub) * 256:(kc * 4 + sub) * 256 + 256],
                        start=(kc == 0 and sub == 0), stop=(kc == j and sub == 3)),
                        "pe" if (kc == j and sub == 3) else None)
            G["last_pv"] = G["pv_par"][gb] = pe_pv
            p.wait("vector", "pe", pe_pv)
            if j >= 2:
                ctx.r["sa"].wait(p, "vector", G["obf_guard"][j % 2])
            dv_oev = op(p, ctx, "vector", lambda e, j=j, g=g, pr=pr, ob=ob: e.tensor_copy(
                obf[:, (j % 2) * H * 128 + (g * 4 + 2 * pr) * 128:
                    (j % 2) * H * 128 + (g * 4 + 2 * pr) * 128 + 256],
                ps_o[:, ob:ob + 256]), "dv")
            G["o_guard"][pp % 2] = dv_oev
            G["pp"] = pp + 1
            return dv_oev

        for j in range(NJ):
            if j >= 2:
                ctx.r["sa"].wait(p, "sync", G["obf_guard"][j % 2])
            if G["last_qk"] is not None:
                p.wait("sync", "pe", G["last_qk"])
            ld_q = ctx.r["lq"].issue(p, "sync", lambda e, j=j: e.dma_start(
                out=qj[:].rearrange("p (h q) -> p h q", h=H),
                in_=d["qf"][j].rearrange("(h hd) q -> hd h q", hd=128)))
            dv_oev = None
            for g in range(8):
                gb = (j * 8 + g) % 2
                kb = gb * NJ * 512
                if gb in G["qk_par"]:
                    p.wait("sync", "pe", G["qk_par"][gb])
                if gb in G["pv_par"]:
                    p.wait("sync", "pe", G["pv_par"][gb])
                ld_k = {}
                ld_v = {}
                for kc in range(j + 1):
                    src_k = d["k_g"].rearrange("(r kv hd) tt -> r kv hd tt", r=4, kv=HKV)[
                        :, g, :, kc * 128:(kc + 1) * 128].rearrange("r hd tt -> hd r tt")
                    ctx.r["kv"].issue(p, "sync", lambda e, kc=kc, src_k=src_k, kb=kb: e.dma_start(
                        out=ksp[:, kb + kc * 512: kb + (kc + 1) * 512], in_=src_k), advance=False)
                    src_v = d["v_g"].rearrange("(r kv tt) hd -> r kv tt hd", r=4, kv=HKV)[
                        :, g, kc * 128:(kc + 1) * 128, :].rearrange("r tt hd -> tt r hd")
                    tok = ctx.r["kv"].issue(p, "sync", lambda e, kc=kc, src_v=src_v, kb=kb: e.dma_start(
                        out=vsp[:, kb + kc * 512: kb + (kc + 1) * 512], in_=src_v))
                    ld_k[kc] = tok
                    ld_v[kc] = tok
                for pr in range(2):
                    dv_oev = attn_pair(j, g, pr, ld_q, ld_k, ld_v, gb)

            p.wait("sync", "dv", dv_oev)
            dst = d["attn"][:, j * 128:(j + 1) * 128].rearrange("(h hd) q -> hd h q", hd=128)
            st_o = ctx.r["sa"].issue(p, "sync", lambda e, dst=dst, j=j: e.dma_start(
                out=dst, in_=obf[:, (j % 2) * H * 128:(j % 2 + 1) * H * 128].rearrange(
                    "p (h q) -> p h q", h=H)))
            G["obf_guard"][j % 2] = st_o
        p.emit()


def emit_oproj_supertile(ctx, st):
    nc, t, d = ctx.nc, ctx.t, ctx.dram
    c0 = st * STW
    tag = f"op{st}"
    with (
        nc.sbuf_tensor(f"asb_{tag}", [128, DC * 512], BF16) as attn_sb,
        nc.sbuf_tensor(f"wos_{tag}", [128, 3 * DC * 128], BF16) as wsb,
        nc.sbuf_tensor(f"x1r_{tag}", [128, 2 * 512], F32) as xr,
        nc.sbuf_tensor(f"oc3_{tag}", [128, 2 * 512], F32) as oc,
        nc.sbuf_tensor(f"sq3_{tag}", [128, 2 * 512], F32) as sqc,
        nc.sbuf_tensor(f"mb3_{tag}", [128, 512], F32) as m_bc,
        nc.sbuf_tensor(f"rb3_{tag}", [128, 512], F32) as r_bc,
        nc.sbuf_tensor(f"rw3_{tag}", [1, 2048], F32) as rows,
        nc.sbuf_tensor(f"xnc3_{tag}", [128, 2 * 512], F32) as xnc,
        nc.sbuf_tensor(f"xno3_{tag}", [128, 2 * 512], BF16) as xno,
        nc.psum_tensor(f"po_{tag}", [128, 1024], F32) as ps_o2,
        nc.psum_tensor(f"p1c_{tag}", [128, 512], F32) as ps_s1,
        nc.psum_tensor(f"p2c_{tag}", [128, 512], F32) as ps_s2,
        nc.psum_tensor(f"pbc_{tag}", [128, 1024], F32) as ps_bc,
    ):
        p = Prog(ctx)
        barrier(p, ctx)
        ld_a = None
        for cc in range(DC):
            ld_a = op(p, ctx, "sync", lambda e, cc=cc: e.dma_start(
                out=attn_sb[:, cc * 512:(cc + 1) * 512],
                in_=d["attn"][cc * 128:(cc + 1) * 128, c0:c0 + STW]), "ld", 16)
        p.wait("tensor", "ld", ld_a)

        wq_m = {}
        pe_mm = {}
        pe_stats = {}
        dv_oc = {}
        ac_sq = {}
        st_oc = {}
        ld_xr = {}

        def emit_stats(sdc, start, stop):
            spb = (sdc % 2) * 512
            p.wait("tensor", "ac", ac_sq[sdc])
            op(p, ctx, "tensor", lambda e, spb=spb: e.matmul(
                ps_s1[0:1, :], t["ones_col"][:], oc[:, spb:spb + 512],
                start=start, stop=stop))
            pe_stats[sdc] = op(p, ctx, "tensor", lambda e, spb=spb: e.matmul(
                ps_s2[0:1, :], t["ones_col"][:], sqc[:, spb:spb + 512],
                start=start, stop=stop), "pe")

        for mc in range(DC):
            pb = (mc % 2) * 512
            wsl = (mc % 3) * DC * 128
            if mc >= 3:
                p.wait("gpsimd", "pe", pe_mm[mc - 3])
            wq_m[mc] = ctx.r["wo3"].issue(p, "gpsimd", lambda e, mc=mc: e.dma_start(
                out=wsb[:, (mc % 3) * DC * 128:(mc % 3) * DC * 128 + DC * 128],
                in_=d["wo_blk"][mc]))

            if mc >= 2:
                p.wait("sync", "dv", dv_oc[mc - 2])
            ld_xr[mc] = ctx.r["xr"].issue(p, "sync", lambda e, mc=mc, pb=pb: e.dma_start(
                out=xr[:, pb:pb + 512], in_=d["x1_fm"][mc * 128:(mc + 1) * 128, c0:c0 + STW]))

            if mc >= 2:
                emit_stats(mc - 2, start=(mc - 2 == 0), stop=False)

            ctx.r["wo3"].wait(p, "tensor", wq_m[mc])
            if mc >= 2:
                p.wait("tensor", "dv", dv_oc[mc - 2])
            for cc in range(DC):
                pe_mm[mc] = op(p, ctx, "tensor", lambda e, wsl=wsl, pb=pb, cc=cc: e.matmul(
                    ps_o2[:, pb:pb + 512], wsb[:, wsl + cc * 128: wsl + cc * 128 + 128],
                    attn_sb[:, cc * 512:(cc + 1) * 512], start=(cc == 0), stop=(cc == DC - 1)),
                    "pe" if cc == DC - 1 else None)

            p.wait("vector", "pe", pe_mm[mc])
            ctx.r["xr"].wait(p, "vector", ld_xr[mc])
            if mc >= 2:
                ctx.r["so"].wait(p, "vector", st_oc[mc - 2])
                p.wait("vector", "pe", pe_stats[mc - 2])
            dv_oc[mc] = op(p, ctx, "vector", lambda e, pb=pb: e.tensor_add(
                oc[:, pb:pb + 512], ps_o2[:, pb:pb + 512], xr[:, pb:pb + 512]), "dv")

            p.wait("scalar", "dv", dv_oc[mc])
            if mc >= 2:
                p.wait("scalar", "pe", pe_stats[mc - 2])
            ac_sq[mc] = op(p, ctx, "scalar", lambda e, pb=pb: e.activation(
                sqc[:, pb:pb + 512], oc[:, pb:pb + 512], AF.Square), "ac")

            p.wait("sync", "dv", dv_oc[mc])
            st_oc[mc] = ctx.r["so"].issue(p, "sync", lambda e, mc=mc, pb=pb: e.dma_start(
                out=d["x2_fm"][mc * 128:(mc + 1) * 128, c0:c0 + STW], in_=oc[:, pb:pb + 512]))

        emit_stats(DC - 2, start=False, stop=False)
        emit_stats(DC - 1, start=False, stop=True)
        dv_bc = emit_ln_finalize(p, ctx, rows, ps_s1, ps_s2, ps_bc, m_bc, r_bc,
                                 pe_stats_done=pe_stats[DC - 1])

        def xn_write(dc):
            sl = (dc % 2) * 512
            return (d["xn3_fm"][dc * 128:(dc + 1) * 128, c0:c0 + STW], xno[:, sl:sl + 512])
        emit_ln_normalize(p, ctx, d["x2_fm"], c0, xn_write, xnc, sqc, m_bc, r_bc, dv_bc,
                          src_waiters=(ctx.r["so"].waiter(st_oc[DC - 2]),
                                       ctx.r["so"].waiter(st_oc[DC - 1]),
                                       sem_waiter("pe", pe_stats[DC - 1])))
        p.emit()


def emit_tail(ctx, debug):
    d = ctx.dram
    p = Prog(ctx)
    barrier(p, ctx, engines=("sync",))
    srcs = {"x1": "x1_fm", "xn2": "xn2_fm", "qf": "qf", "k_in": "k_in", "v_in": "v_in",
            "k_g": "k_g", "attn": "attn", "x2": "x2_fm", "xn3": "xn3_fm"}
    barrier(p, ctx, engines=("gpsimd",))
    for name in debug:
        op(p, ctx, "gpsimd", lambda e, a=d["dbg_" + name], b_=d[srcs[name]]: e.dma_start(
            out=a[:], in_=b_[:]), "st", 16)
    p.wait("sync", "st", ctx.val("st"))
    p.emit()



def _stage_shared(L0_attn_Wq, L0_attn_Wk, L0_attn_Wv, L0_attn_Wo,
                  L0_attn_q_norm_w, L0_attn_k_norm_w,
                  L0_ffn_gate_w, L0_ffn_gate_s, L0_ffn_up_w, L0_ffn_up_s,
                  L0_ffn_down_w, L0_ffn_down_s):
    gate_wT = np.ascontiguousarray(np.asarray(L0_ffn_gate_w).T)
    up_wT = np.ascontiguousarray(np.asarray(L0_ffn_up_w).T)
    down_wT = np.ascontiguousarray(np.asarray(L0_ffn_down_w).T)

    def blk_a(wT):
        return np.ascontiguousarray(
            wT.reshape(DC, 128, IC, 128).transpose(2, 1, 0, 3).reshape(IC, 128, DC * 128)
        ).astype(np.int8)

    gate_blk = blk_a(gate_wT)
    up_blk = blk_a(up_wT)
    down_blk = np.ascontiguousarray(
        down_wT.reshape(IC, 128, DC, 128).transpose(2, 1, 0, 3).reshape(DC, 128, IC * 128)
    ).astype(np.int8)

    qkv_wT = np.ascontiguousarray(np.concatenate(
        [np.asarray(L0_attn_Wq).T, np.asarray(L0_attn_Wk).T, np.asarray(L0_attn_Wv).T],
        axis=1)).astype(np.float32)
    qkv_blk = np.ascontiguousarray(
        qkv_wT.reshape(DC, 128, 48, 128).transpose(2, 1, 0, 3).reshape(48, 128, DC * 128)
    ).astype(np.float32)
    WoT = np.ascontiguousarray(np.asarray(L0_attn_Wo).T)
    wo_blk = np.ascontiguousarray(
        WoT.reshape(DC, 128, DC, 128).transpose(2, 1, 0, 3).reshape(DC, 128, DC * 128)
    ).astype(np.float32)

    qn_bc = np.ascontiguousarray(np.asarray(L0_attn_q_norm_w, dtype=np.float32)[:, None])
    kn_bc = np.ascontiguousarray(np.asarray(L0_attn_k_norm_w, dtype=np.float32)[:, None])
    scales = np.ascontiguousarray(np.broadcast_to(
        np.array([float(np.asarray(L0_ffn_gate_s)[0]), float(np.asarray(L0_ffn_up_s)[0]),
                  float(np.asarray(L0_ffn_down_s)[0])], dtype=np.float32)[None, :], (128, 3)))
    return dict(gate_blk=gate_blk, up_blk=up_blk, down_blk=down_blk, qkv_blk=qkv_blk,
                wo_blk=wo_blk, qn_bc=qn_bc, kn_bc=kn_bc, scales=scales)


def _templates(c):
    jj = np.arange(512)[None, :]
    ii = np.arange(128)[:, None]
    out = np.empty((NJ, 128, 512), dtype=np.float32)
    for dd in range(NJ):
        rel = jj - ii - 512 * dd - 128 * c
        out[dd] = np.where(rel <= 0, rel, MASKVAL).astype(np.float32)
    return out


def _token_index(c):
    j = np.arange(NJ)[:, None]
    i = np.arange(128)[None, :]
    return (c * 128 + j * 512 + i).reshape(-1)


def kernel(L0_attn_x, L0_attn_Wq, L0_attn_Wk, L0_attn_Wv, L0_attn_Wo,
           L0_attn_q_norm_w, L0_attn_k_norm_w,
           L0_ffn_gate_w, L0_ffn_gate_s, L0_ffn_up_w, L0_ffn_up_s,
           L0_ffn_down_w, L0_ffn_down_s):
    x = np.asarray(L0_attn_x, dtype=np.float32)
    shared = _stage_shared(L0_attn_Wq, L0_attn_Wk, L0_attn_Wv, L0_attn_Wo,
                           L0_attn_q_norm_w, L0_attn_k_norm_w,
                           L0_ffn_gate_w, L0_ffn_gate_s, L0_ffn_up_w, L0_ffn_up_s,
                           L0_ffn_down_w, L0_ffn_down_s)
    nc = build_nc()
    in_maps = []
    for core in range(NCORE):
        s, c = divmod(core, 4)
        idx = _token_index(c)
        x_fm = np.ascontiguousarray(x[s][idx].T)
        m = dict(shared)
        m["x_fm"] = x_fm
        m["tmpl"] = _templates(c)
        in_maps.append(m)
    res = run_bass_kernel_spmd(nc, in_maps, core_ids=list(range(NCORE)))
    out = np.empty((B, S, DM), dtype=np.float32)
    for core in range(NCORE):
        s, c = divmod(core, 4)
        idx = _token_index(c)
        out[s][idx] = res.results[core]["out"].T
    return out
